# version 06
# baseline (speedup 1.0000x reference)
import numpy as np
import ml_dtypes
import concourse.bass as bass
import concourse.mybir as mybir
from concourse.bass_utils import run_bass_kernel_spmd

F32 = mybir.dt.float32
BF16 = mybir.dt.bfloat16
I32 = mybir.dt.int32
AF = mybir.ActivationFunctionType
ALU = mybir.AluOpType
AX = mybir.AxisListType

P = 128
D = 4096
TOK = 2048
NT = TOK // P
KC = D // P
INTER = 11008
NCH = INTER // P
EPS = 1e-5
PI = float(np.pi)

DCH = []
for _kb in range(11):
    for _j in range(8):
        DCH.append((_kb, _j, 128 if _kb < 10 else 96))
NDCH = len(DCH)
NGRP = NDCH // 4
WD_UNITS = [(kb, o) for kb in range(11) for o in range(8)]
NWU = len(WD_UNITS)

SINC = [1.0, -1.0 / 6, 1.0 / 120, -1.0 / 5040, 1.0 / 362880, -1.0 / 39916800,
        1.0 / 6227020800.0]
COSC = [1.0, -0.5, 1.0 / 24, -1.0 / 720, 1.0 / 40320, -1.0 / 3628800,
        1.0 / 479001600.0]


def build_program(stage=4, debug=False):
    ST = 2 if stage == 15 else stage
    nc = bass.Bass("TRN2", target_bir_lowering=False)

    def dram_in(name, shape, dt):
        return nc.dram_tensor(name, list(shape), dt, kind="ExternalInput")

    x_h = dram_in("L0_attn_u", [TOK, D], F32)
    lam_h = dram_in("L0_attn_Lambda_ri", [64, 2], F32)
    p_h = dram_in("L0_attn_p_ri", [64, 2], F32)
    q_h = dram_in("L0_attn_q_ri", [64, 2], F32)
    vc_h = dram_in("L0_attn_Vc_ri", [64, 128], F32)
    ct_h = dram_in("L0_attn_Ct_ri", [1, 128], F32)
    bp_h = dram_in("L0_attn_Bp", [64, 1], F32)
    d_h = dram_in("L0_attn_D", [1], F32)
    ls_h = dram_in("L0_attn_log_step", [1], F32)
    qwg_h = dram_in("L0_ffn_gate_qweight", [512, INTER], I32)
    sg_h = dram_in("L0_ffn_gate_scales", [32, INTER], F32)
    zg_h = dram_in("L0_ffn_gate_qzeros", [32, 1376], I32)
    qwu_h = dram_in("L0_ffn_up_qweight", [512, INTER], I32)
    su_h = dram_in("L0_ffn_up_scales", [32, INTER], F32)
    zu_h = dram_in("L0_ffn_up_qzeros", [32, 1376], I32)
    qwd_h = dram_in("L0_ffn_down_qweight", [1376, D], I32)
    sd_h = dram_in("L0_ffn_down_scales", [86, D], F32)
    zd_h = dram_in("L0_ffn_down_qzeros", [86, 512], I32)
    dram_in("L0_ffn_gate_g_idx", [D], I32)
    dram_in("L0_ffn_up_g_idx", [D], I32)
    dram_in("L0_ffn_down_g_idx", [INTER], I32)

    out_h = nc.dram_tensor("out", [TOK, D], F32, kind="ExternalOutput")
    sk = "ExternalOutput" if debug else "Internal"
    x1_h = nc.dram_tensor("x1_scr", [TOK, D], F32, kind=sk)
    kpad_h = nc.dram_tensor("kpad_scr", [4224], BF16, kind=sk)
    kst_h = nc.dram_tensor("kstage_scr", [4353 * 127 + 4224], BF16, kind="Internal")
    zcg_h = nc.dram_tensor("zcg_scr", [32, INTER], BF16, kind="Internal")
    zcu_h = nc.dram_tensor("zcu_scr", [32, INTER], BF16, kind="Internal")
    zcd_h = nc.dram_tensor("zcd_scr", [86, D], BF16, kind="Internal")
    hT_h = nc.dram_tensor("hT_scr", [INTER, TOK], BF16, kind=sk)
    wd_h = nc.dram_tensor("wd_scr", [8 * NDCH * P * 512], BF16, kind="Internal")
    arep = {"g": nc.dram_tensor("arep_g", [512, INTER], F32, kind="Internal"),
            "u": nc.dram_tensor("arep_u", [512, INTER], F32, kind="Internal")}
    zrep = {"g": nc.dram_tensor("zrep_g", [512, INTER], BF16, kind="Internal"),
            "u": nc.dram_tensor("zrep_u", [512, INTER], BF16, kind="Internal")}
    osc_h = nc.dram_tensor("osc_scr", [TOK, D], F32, kind=sk)
    dbg_h = nc.dram_tensor("dbg_scr", [64, 64 * 52], F32, kind=sk) if debug else None

    base = (nc.sbuf_base + 31) // 32 * 32
    top = nc.sbuf_top

    def sba(name, shape, dt, at):
        off = base + at
        nbytes = int(np.prod(shape[1:])) * mybir.dt.size(dt)
        assert off + nbytes <= top, (name, off + nbytes - top)
        return nc.alloc_sbuf_tensor_at(name, list(shape), dt, offset=off)

    acur = [0]
    def sbA(name, shape, dt):
        nbytes = int(np.prod(shape[1:])) * mybir.dt.size(dt)
        off = (acur[0] + 31) // 32 * 32
        acur[0] = off + nbytes
        return sba(name, shape, dt, off)

    ident_f = sbA("ident_f", [P, P], F32)
    ident_b = sbA("ident_b", [P, P], BF16)
    ones_f = sbA("ones_f", [P, P], F32)
    ones_b = sbA("ones_b", [P, P], BF16)
    onesr = sbA("onesr", [1, P], F32)
    dcol = sbA("dcol", [P, 1], F32)
    g2s = sbA("g2s", [64, 1], F32)
    step_t = sbA("step_t", [1, 1], F32)
    inv2_t = sbA("inv2_t", [1, 1], F32)
    kpz = sbA("kpz", [1, 128], BF16)
    bcol = sbA("bcol", [64, 2], F32)
    cst = sbA("cst", [P, 3], F32)
    toep = sbA("toep", [P, 35 * 128], BF16)
    stats = {nm: [sbA(f"st_{nm}_{i}", [P, 6], F32) for i in range(2)]
             for nm in ("s1", "s2", "s3", "s4")}
    assert acur[0] < 13504, acur[0]

    B0 = 13504
    xbuf = [sba(f"xbuf{i}", [P, D], F32, B0 + i * 16384) for i in range(2)]
    u_f = [sba(f"u_f{i}", [P, D], F32, B0 + 32768 + i * 16384) for i in range(2)]
    ubf = [sba(f"ubf{i}", [P, D], BF16, B0 + 65536 + i * 8192) for i in range(2)]
    uT = [sba(f"uT{i}", [P, D], BF16, B0 + 81920 + i * 8192) for i in range(2)]
    y_sb = [sba(f"y_sb{i}", [P, D], F32, B0 + 98304 + i * 16384) for i in range(2)]
    ln2n = sba("ln2n", [P, D], F32, B0 + 131072)
    x1b = [sba(f"x1b{i}", [P, D], F32, B0 + 147456 + i * 16384) for i in range(2)]

    pcur = [B0 + 32768]
    def sbP(name, shape, dt):
        nbytes = int(np.prod(shape[1:])) * mybir.dt.size(dt)
        off = (pcur[0] + 31) // 32 * 32
        pcur[0] = off + nbytes
        return sba(name, shape, dt, off)

    c64 = sbP("c64", [64, 64], F32)
    s64 = sbP("s64", [64, 64], F32)
    twc = sbP("twc", [64, 64], F32)
    tws = sbP("tws", [64, 64], F32)
    cosB = sbP("cosB", [64, 64], F32)
    sinB = sbP("sinB", [64, 64], F32)
    colidx = sbP("colidx", [64, 64], I32)
    rowf = sbP("rowf", [64, 1], F32)
    colf = sbP("colf", [64, 64], F32)
    pA = sbP("pA", [64, 64], F32)
    pB = sbP("pB", [64, 64], F32)
    pC = sbP("pC", [64, 64], F32)
    pD = sbP("pD", [64, 64], F32)
    pE = sbP("pE", [64, 64], F32)
    pF = sbP("pF", [64, 64], F32)
    omc = sbP("omc", [64, 64], F32)
    oms = sbP("oms", [64, 64], F32)
    gre = sbP("gre", [64, 64], F32)
    gim = sbP("gim", [64, 64], F32)
    cre = sbP("cre", [64, 64], F32)
    cim = sbP("cim", [64, 64], F32)
    lam_s = sbP("lam_s", [1, 128], F32)
    pq_s = sbP("pq_s", [1, 256], F32)
    vc_s = sbP("vc_s", [64, 128], F32)
    ct_s = sbP("ct_s", [1, 128], F32)
    bp_s = sbP("bp_s", [64, 1], F32)
    dls_s = sbP("dls_s", [1, 2], F32)
    vct = sbP("vct", [64, 128], F32)
    brow = sbP("brow", [1, 128], F32)
    vrow = sbP("vrow", [1, 512], F32)
    vrep = sbP("vrep", [64, 512], F32)
    lrep = sbP("lrep", [64, 128], F32)
    kk = sbP("kk", [64, 512], F32)
    aru = sbP("aru", [64, 128], F32)
    m1s = sbP("m1s", [64, 128], F32)
    m2s = sbP("m2s", [64, 128], F32)
    yre = sbP("yre", [64, 64], F32)
    yim = sbP("yim", [64, 64], F32)
    zre = sbP("zre", [64, 64], F32)
    zim = sbP("zim", [64, 64], F32)
    zret = sbP("zret", [64, 64], F32)
    zimtn = sbP("zimtn", [64, 64], F32)
    ktb = sbP("ktb", [64, 64], BF16)
    dre3_off = (pcur[0] + 31) // 32 * 32 - 0
    dre3 = sbP("dre3", [64, 4096], F32)
    dim3 = sbP("dim3", [64, 4096], F32)
    rre3 = sbP("rre3", [64, 4096], F32)
    rim3 = sbP("rim3", [64, 4096], F32)
    grid = sbP("grid", [64, 4096], F32)
    qzg_s = sbP("qzg_s", [32, 1376], I32)
    qzu_s = sbP("qzu_s", [32, 1376], I32)
    qzd_s = sbP("qzd_s", [86, 512], I32)
    zci = sba("zci", [86, INTER], I32, dre3_off)
    zcb = sba("zcb", [86, INTER], BF16, dre3_off + 44032)
    assert pcur[0] < B0 + 180224, pcur[0]

    x1r = [sba(f"x1r{i}", [P, D], F32, B0 + i * 16384) for i in range(2)]
    x1nb = [sba(f"x1nb{i}", [P, D], BF16, B0 + 32768 + i * 8192) for i in range(2)]
    XT0 = 81728
    xt = sba("xt", [P, KC * TOK], BF16, XT0)

    dcur = [B0]
    def sbD(name, shape, dt):
        nbytes = int(np.prod(shape[1:])) * mybir.dt.size(dt)
        off = (dcur[0] + 31) // 32 * 32
        dcur[0] = off + nbytes
        return sba(name, shape, dt, off)

    qwb = {m: [sbD(f"qwb{m}{i}", [P, 512], I32) for i in range(2)] for m in "gu"}
    ab = {m: [sbD(f"ab{m}{i}", [P, 512], F32) for i in range(2)] for m in "gu"}
    zcbuf = {m: [sbD(f"zcbf{m}{i}", [P, 512], BF16) for i in range(2)] for m in "gu"}
    w4i = [sbD(f"w4i{i}", [P, 512], I32) for i in range(2)]
    wfb = [sbD(f"wfb{i}", [P, 512], F32) for i in range(2)]
    wtil = {m: sbD(f"wtil{m}", [P, D], BF16) for m in "gu"}
    silu_sb = [sbD(f"silu{i}", [P, 512], F32) for i in range(2)]
    hTb = [sbD(f"hTb{i}", [P, 512], BF16) for i in range(2)]
    qwdb = [sbD(f"qwdb{i}", [P, 512], I32) for i in range(2)]
    adb = [sbD(f"adb{i}", [P, 512], F32) for i in range(2)]
    zcdb = [sbD(f"zcdb{i}", [P, 512], BF16) for i in range(2)]
    w4d = sbD("w4d", [P, 512], I32)
    wfd = sbD("wfd", [P, 512], F32)
    wdo = [sbD(f"wdo{i}", [P, 512], BF16) for i in range(2)]
    assert dcur[0] <= XT0, dcur[0]

    htblk = sba("htblk", [P, NDCH * 1024], BF16, B0)
    E1 = B0 + NDCH * 1024 * 2
    wdbuf = [sba(f"wdbuf{i}", [P, 2048], BF16, E1 + i * 4096) for i in range(2)]
    slab = [sba(f"slab{i}", [P, 512], F32, E1 + 8192 + i * 2048) for i in range(2)]

    osb = [sba(f"osb{i}", [P, D], F32, B0 + i * 16384) for i in range(2)]
    x1c = [sba(f"x1c{i}", [P, D], F32, B0 + 32768 + i * 16384) for i in range(2)]
    ln4n = sba("ln4n", [P, D], F32, B0 + 65536)
    fout = [sba(f"fout{i}", [P, D], F32, B0 + 81920 + i * 16384) for i in range(2)]

    pbg = [nc.psum_tensor(f"pb{i}", [P, 512], F32).__enter__() for i in range(8)]
    def pbank(i):
        return pbg[i][:]
    def pbank_bf(i):
        return pbg[i][:].bitcast(BF16)

    sems = {}
    for nm in ["d_pro", "d_x0", "d_x1", "d_x1r0", "d_x1r1", "d_qw0", "d_qw1",
               "d_azc0", "d_azc1", "d_rep", "d_qd0", "d_qd1", "d_toep", "d_kpad", "d_kst",
               "d_zcw", "d_x1w0", "d_x1w1", "d_hw0", "d_hw1", "d_wdw0", "d_wdw1",
               "d_hr0", "d_hr1", "d_hr2", "d_hr3", "d_hr4", "d_hr5", "d_hr6", "d_hr7", "d_wdr0", "d_wdr1", "d_slab0", "d_slab1",
               "d_osr0", "d_osr1", "d_out0", "d_out1", "pg", "ps", "pv", "pt",
               "c_v1", "c_u", "c_ub", "c_tr1a", "c_uTe", "c_conv", "c_yassb",
               "c_v2", "c_ln2", "c_x1", "c_v3", "c_ln3", "c_tr1b", "c_xTe",
               "c_dqg", "c_dqu", "c_wfg", "c_wfu", "c_psg", "c_psu",
               "c_silu", "c_hT", "c_qfree", "c_dqd",
               "c_mm3", "c_ps3", "c_slabv", "c_v4", "c_ln4", "c_fo",
               "vself", "sself", "gself"]:
        sems[nm] = nc.alloc_semaphore(name=nm)
    S = type("S", (), sems)

    nc.const_aps.register(F32, EPS, cst[:, 0:1])
    nc.const_aps.register(F32, 1.0 / D, cst[:, 1:2])
    nc.const_aps.register(F32, -0.5, cst[:, 2:3])

    hwm = {}
    def W(eng, tag, sem, val):
        if val <= 0:
            return
        key = (tag, sem.name)
        if hwm.get(key, -1) >= val:
            return
        hwm[key] = val
        eng.wait_ge(sem, val)

    counts = {}
    def bump(sem, amt=1):
        counts[sem.name] = counts.get(sem.name, 0) + amt
        return (sem, counts[sem.name])

    dzw_n = [3]
    blk = nc.Block().__enter__()

    @blk.gpsimd
    def _(g):
        g.iota(colidx[:], [[1, 64]], channel_multiplier=0)
        g.iota(rowf[:], [[0, 1]], channel_multiplier=1,
               allow_small_or_imprecise_dtypes=True)
        g.memset(ones_f[:], 1.0)
        g.memset(ones_b[:], 1.0)
        g.memset(onesr[:], 1.0)
        g.memset(kpz[:], 0.0)
        g.memset(cst[:, 1:2], 1.0 / D)
        g.memset(cst[:, 2:3], -0.5)
        g.memset(cst[:, 0:1], EPS).then_inc(S.gself, 1)
        g.wait_ge(S.gself, 1)
        g.affine_select(out=ident_f[:], in_=ones_f[:], pattern=[[-1, 128]],
                        compare_op=ALU.is_equal, fill=0.0, base=0,
                        channel_multiplier=1)
        g.affine_select(out=ident_b[:], in_=ones_b[:], pattern=[[-1, 128]],
                        compare_op=ALU.is_equal, fill=0.0, base=0,
                        channel_multiplier=1).then_inc(S.pg, 1)

        g.wait_ge(S.pv, 12)
        g.dma_start(out=bass.AP(kpad_h, 0, [[1, 128]]),
                    in_=kpz[0:1, :]).then_inc(S.d_kpad, 16)
        g.dma_start(out=bass.AP(kpad_h, 128, [[64, 64], [1, 64]]),
                    in_=ktb[:]).then_inc(S.d_kpad, 16)
        g.wait_ge(S.d_kpad, 32)
        for pp in range(128):
            g.dma_start(out=bass.AP(kst_h, 4353 * pp, [[1, 4224]]),
                        in_=bass.AP(kpad_h, 0, [[1, 4224]])).then_inc(S.d_kst, 16)
        for m, sh in (("g", sg_h), ("u", su_h)):
            for bb in range(4):
                g.dma_start(out=bass.AP(arep[m], bb * 128 * INTER,
                                        [[INTER, 128], [1, INTER]]),
                            in_=bass.AP(sh, 8 * bb * INTER,
                                        [[INTER, 8], [0, 16], [1, INTER]])
                            ).then_inc(S.d_rep, 16)
        g.wait_ge(S.pv, 13)
        g.dma_start(out=zcg_h.ap(), in_=zcb[0:32, :]).then_inc(S.d_zcw, 16)
        g.wait_ge(S.pv, 14)
        g.dma_start(out=zcu_h.ap(), in_=zcb[0:32, :]).then_inc(S.d_zcw, 16)
        g.wait_ge(S.pv, 15)
        g.dma_start(out=zcd_h.ap(), in_=zcb[0:86, 0:D]).then_inc(S.d_zcw, 16)
        for m, zh in (("g", zcg_h), ("u", zcu_h)):
            g.wait_ge(S.d_zcw, 16 if m == "g" else 32)
            for bb in range(4):
                g.dma_start(out=bass.AP(zrep[m], bb * 128 * INTER,
                                        [[INTER, 128], [1, INTER]]),
                            in_=bass.AP(zh, 8 * bb * INTER,
                                        [[INTER, 8], [0, 16], [1, INTER]])
                            ).then_inc(S.d_rep, 16)
        if debug:
            slots = [c64, s64, twc, tws, cosB, sinB, omc, oms, gre, gim, cre,
                     cim, None, None, None, None, None, None, None, None,
                     None, None, zre, zim, vrep, None, None, None]
            for i, tl in enumerate(slots):
                if tl is None:
                    continue
                w = tl[:].shape[1]
                g.dma_start(out=bass.AP(dbg_h, i * 64, [[64 * 52, 64], [1, w]]),
                            in_=tl[0:64, 0:w]).then_inc(S.d_zcw, 16)
                dzw_n[0] += 1
            g.dma_start(out=bass.AP(dbg_h, 12 * 64, [[64 * 52, 64], [1, 128]]),
                        in_=aru[:]).then_inc(S.d_zcw, 16)
            dzw_n[0] += 1
            for i in range(8):
                g.dma_start(out=bass.AP(dbg_h, (14 + i) * 64, [[64 * 52, 64], [1, 64]]),
                            in_=kk[:, i * 64:(i + 1) * 64]).then_inc(S.d_zcw, 16)
                dzw_n[0] += 1
            g.dma_start(out=bass.AP(dbg_h, 36 * 64, [[64 * 52, 1], [1, 128]]),
                        in_=brow[0:1, :]).then_inc(S.d_zcw, 16)
            dzw_n[0] += 1
            g.dma_start(out=bass.AP(dbg_h, 40 * 64, [[64 * 52, 1], [1, 512]]),
                        in_=vrow[0:1, 0:512]).then_inc(S.d_zcw, 16)
            dzw_n[0] += 1
        if ST < 2:
            return

        for t in range(NT):
            g.wait_ge(S.c_x1, t + 1)
            W(g, "g", [S.d_x1w0, S.d_x1w1][t % 2], 16 * (t // 2))
            g.dma_start(out=bass.AP(x1_h, t * P * D, [[D, P], [1, D]]),
                        in_=x1b[t % 2][:]).then_inc([S.d_x1w0, S.d_x1w1][t % 2], 16)
        if ST < 3:
            return

        def hts(ncx, tg):
            g.wait_ge(S.c_hT, 4 * ncx + tg + 1)
            W(g, "g", [S.d_hw0, S.d_hw1][tg % 2], 16 * (2 * ncx + tg // 2))
            g.dma_start(out=bass.AP(hT_h, (ncx * P) * TOK + tg * 512,
                                    [[TOK, P], [1, 512]]),
                        in_=hTb[tg % 2][:]).then_inc([S.d_hw0, S.d_hw1][tg % 2], 16)

        def wds(u_i, j):
            kb, o = WD_UNITS[u_i]
            cd = kb * 8 + j
            Kp = DCH[cd][2]
            g.wait_ge(S.c_dqd, u_i * 8 + j + 1)
            W(g, "g", [S.d_wdw0, S.d_wdw1][j % 2], 16 * (u_i * 4 + j // 2))
            g.dma_start(out=bass.AP(wd_h, o * NDCH * P * 512 + cd * P * 512,
                                    [[512, Kp], [1, 512]]),
                        in_=wdo[j % 2][0:Kp, :]).then_inc([S.d_wdw0, S.d_wdw1][j % 2], 16)

        for ncx in range(NCH):
            if ncx > 0:
                for tg in range(4):
                    hts(ncx - 1, tg)
            if ncx < NWU:
                for j in range(8):
                    wds(ncx, j)
        for j in range(8):
            wds(NWU - 2, j)
        for j in range(8):
            wds(NWU - 1, j)
        for tg in range(4):
            hts(NCH - 1, tg)
        if ST < 4:
            return

        for b in range(2):
            for mg in range(8):
                for tt in range(8):
                    idx = (b * 8 + mg) * 8 + tt
                    g.wait_ge(S.c_slabv, idx + 1)
                    W(g, "g", [S.d_slab0, S.d_slab1][idx % 2], 16 * (idx // 2))
                    g.dma_start(out=bass.AP(osc_h, (b * 1024 + tt * P) * D + mg * 512,
                                            [[D, P], [1, 512]]),
                                in_=slab[idx % 2][:]).then_inc([S.d_slab0, S.d_slab1][idx % 2], 16)

        for t in range(NT):
            g.wait_ge(S.c_fo, t + 1)
            W(g, "g", [S.d_out0, S.d_out1][t % 2], 16 * (t // 2))
            g.dma_start(out=bass.AP(out_h, t * P * D, [[D, P], [1, D]]),
                        in_=fout[t % 2][:]).then_inc([S.d_out0, S.d_out1][t % 2], 16)

    @blk.sync
    def _(sy):
        for ap_dst, src in [
            (lam_s[0:1, :], bass.AP(lam_h, 0, [[1, 128]])),
            (pq_s[0:1, 0:128], bass.AP(p_h, 0, [[1, 128]])),
            (pq_s[0:1, 128:256], bass.AP(q_h, 0, [[1, 128]])),
            (vc_s[:], vc_h.ap()),
            (ct_s[0:1, :], bass.AP(ct_h, 0, [[1, 128]])),
            (bp_s[:], bp_h.ap()),
            (dls_s[0:1, 0:1], bass.AP(d_h, 0, [[1, 1]])),
            (dls_s[0:1, 1:2], bass.AP(ls_h, 0, [[1, 1]])),
            (qzg_s[:], zg_h.ap()),
            (qzu_s[:], zu_h.ap()),
            (qzd_s[:], zd_h.ap()),
        ]:
            sy.dma_start(out=ap_dst, in_=src).then_inc(S.d_pro, 16)

        if ST < 2:
            sy.wait_ge(S.d_toep, 16 * 32)
            return

        for t in range(NT):
            W(sy, "sy", S.c_x1, t - 1)
            W(sy, "sy", [S.d_x0, S.d_x1][t % 2], 16 * (t // 2))
            sy.dma_start(out=xbuf[t % 2][:],
                         in_=bass.AP(x_h, t * P * D, [[D, P], [1, D]])
                         ).then_inc([S.d_x0, S.d_x1][t % 2], 16)

        if stage == 15:
            sy.wait_ge(S.d_x1w0, 16 * NT // 2)
            sy.wait_ge(S.d_x1w1, 16 * NT // 2)
            return
        W(sy, "sy", S.c_x1, NT)
        sy.wait_ge(S.d_x1w0, 16 * NT // 2)
        sy.wait_ge(S.d_x1w1, 16 * NT // 2)
        for t in range(NT):
            W(sy, "sy", S.c_ln3, t - 1)
            W(sy, "sy", [S.d_x1r0, S.d_x1r1][t % 2], 16 * (t // 2))
            sy.dma_start(out=x1r[t % 2][:],
                         in_=bass.AP(x1_h, t * P * D, [[D, P], [1, D]])
                         ).then_inc([S.d_x1r0, S.d_x1r1][t % 2], 16)
        if ST < 3:
            return

        sy.wait_ge(S.c_ln3, NT)
        sy.wait_ge(S.c_tr1b, 4 * NT)
        sy.wait_ge(S.d_zcw, 16 * dzw_n[0])

        def wd_unit_loads(u_i):
            kb, o = WD_UNITS[u_i]
            Kp = 128 if kb < 10 else 96
            ng = Kp // 16
            W(sy, "sy", S.c_dqd, 8 * (u_i - 1))
            W(sy, "sy", [S.d_qd0, S.d_qd1][u_i % 2], 48 * (u_i // 2))
            dq = [S.d_qd0, S.d_qd1][u_i % 2]
            sy.dma_start(out=qwdb[u_i % 2][0:Kp, :],
                         in_=bass.AP(qwd_h, (128 * kb) * D + o * 512,
                                     [[D, Kp], [1, 512]])).then_inc(dq, 16)
            sy.dma_start(out=adb[u_i % 2][0:Kp, :],
                         in_=bass.AP(sd_h, (8 * kb) * D + o * 512,
                                     [[D, ng], [0, 16], [1, 512]])
                         ).then_inc(dq, 16)
            sy.dma_start(out=zcdb[u_i % 2][0:Kp, :],
                         in_=bass.AP(zcd_h, (8 * kb) * D + o * 512,
                                     [[D, ng], [0, 16], [1, 512]])
                         ).then_inc(dq, 16)

        for ncx in range(NCH):
            W(sy, "sy", S.c_dqu, ncx - 1)
            W(sy, "sy", [S.d_qw0, S.d_qw1][ncx % 2], 32 * (ncx // 2))
            for m, qh in (("g", qwg_h), ("u", qwu_h)):
                sy.dma_start(out=qwb[m][ncx % 2][:],
                             in_=bass.AP(qh, ncx * 128,
                                         [[INTER, 128], [128 * INTER, 4], [1, 128]])
                             ).then_inc([S.d_qw0, S.d_qw1][ncx % 2], 16)
            if ncx == 0:
                sy.wait_ge(S.d_rep, 16 * 16)
            W(sy, "sy", S.c_dqg, ncx - 1)
            W(sy, "sy", S.c_dqu, ncx - 1)
            W(sy, "sy", [S.d_azc0, S.d_azc1][ncx % 2], 64 * (ncx // 2))
            dz = [S.d_azc0, S.d_azc1][ncx % 2]
            for m in ("g", "u"):
                sy.dma_start(out=ab[m][ncx % 2][:],
                             in_=bass.AP(arep[m], ncx * 128,
                                         [[INTER, 128], [128 * INTER, 4], [1, 128]])
                             ).then_inc(dz, 16)
                sy.dma_start(out=zcbuf[m][ncx % 2][:],
                             in_=bass.AP(zrep[m], ncx * 128,
                                         [[INTER, 128], [128 * INTER, 4], [1, 128]])
                             ).then_inc(dz, 16)
            if ncx < NWU:
                wd_unit_loads(ncx)
        wd_unit_loads(NWU - 2)
        wd_unit_loads(NWU - 1)
        if ST < 4:
            sy.wait_ge(S.d_hw0, 16 * 2 * NCH)
            sy.wait_ge(S.d_hw1, 16 * 2 * NCH)
            sy.wait_ge(S.d_wdw0, 16 * 4 * NWU)
            sy.wait_ge(S.d_wdw1, 16 * 4 * NWU)
            return

        sy.wait_ge(S.c_wfu, NCH)
        sy.wait_ge(S.c_dqu, NCH)
        sy.wait_ge(S.c_hT, 4 * NCH)
        sy.wait_ge(S.c_dqd, 8 * NWU)
        sy.wait_ge(S.d_wdw0, 16 * 4 * NWU)
        sy.wait_ge(S.d_wdw1, 16 * 4 * NWU)
        sy.wait_ge(S.d_hw0, 16 * 2 * NCH)
        sy.wait_ge(S.d_hw1, 16 * 2 * NCH)
        for b in range(2):
            for cd, (kb, j, Kp) in enumerate(DCH):
                if b == 1:
                    g4 = cd // 4
                    if g4 <= NGRP - 2:
                        W(sy, "sy", S.c_mm3, 7 * (NGRP - 1) + g4 + 1)
                    else:
                        W(sy, "sy", S.c_ps3, 64)
                sy.dma_start(out=htblk[0:Kp, cd * 1024:(cd + 1) * 1024],
                             in_=bass.AP(hT_h, (1024 * kb + j) * TOK + b * 1024,
                                         [[8 * TOK, Kp], [1, 1024]])
                             ).then_inc(getattr(S, f"d_hr{min(cd // 11, 7)}"), 16)
            for mg in range(8):
                for grp in range(NGRP):
                    gidx = (b * 8 + mg) * NGRP + grp
                    if gidx >= 2:
                        pmgi, pgrp = divmod(gidx - 2, NGRP)
                        if pgrp <= NGRP - 2:
                            W(sy, "sy", S.c_mm3, pmgi * (NGRP - 1) + pgrp + 1)
                        else:
                            W(sy, "sy", S.c_ps3, 8 * (pmgi + 1))
                    W(sy, "sy", [S.d_wdr0, S.d_wdr1][gidx % 2],
                      16 * (gidx // 2))
                    sy.dma_start(out=wdbuf[grp % 2][:],
                                 in_=bass.AP(wd_h,
                                             mg * NDCH * P * 512 + grp * 4 * P * 512,
                                             [[512, 128], [P * 512, 4], [1, 512]])
                                 ).then_inc([S.d_wdr0, S.d_wdr1][gidx % 2], 16)

        sy.wait_ge(S.c_ps3, 128)
        sy.wait_ge(S.d_slab0, 16 * 64)
        sy.wait_ge(S.d_slab1, 16 * 64)
        for t in range(NT):
            W(sy, "sy", S.c_fo, t - 1)
            W(sy, "sy", [S.d_osr0, S.d_osr1][t % 2], 32 * (t // 2))
            sy.dma_start(out=osb[t % 2][:],
                         in_=bass.AP(osc_h, t * P * D, [[D, P], [1, D]])
                         ).then_inc([S.d_osr0, S.d_osr1][t % 2], 16)
            sy.dma_start(out=x1c[t % 2][:],
                         in_=bass.AP(x1_h, t * P * D, [[D, P], [1, D]])
                         ).then_inc([S.d_osr0, S.d_osr1][t % 2], 16)
        sy.wait_ge(S.d_out0, 16 * NT // 2)
        sy.wait_ge(S.d_out1, 16 * NT // 2)

    @blk.scalar
    def _(sc):
        def sr(reads, inst_f, inc=None):
            for tk in reads:
                if tk:
                    W(sc, "sc", tk[0], tk[1])
            inst = inst_f()
            if inc is None:
                tick = bump(S.sself)
                inst.then_inc(S.sself, 1)
            else:
                tick = bump(inc[0], inc[1])
                inst.then_inc(inc[0], inc[1])
            return tick

        sc.wait_ge(S.d_pro, 16 * 11)
        sc.wait_ge(S.pg, 1)
        sr([], lambda: sc.activation(out=step_t[:], in_=dls_s[0:1, 1:2],
                                     func=AF.Exp), inc=(S.ps, 1))
        sc.wait_ge(S.d_kst, 16 * 128)
        for d_ in range(32):
            sc.dma_start(out=toep[:, (3 + d_) * 128:(4 + d_) * 128],
                         in_=bass.AP(kst_h, 128 + 128 * d_, [[4352, 128], [1, 128]])
                         ).then_inc(S.d_toep, 16)

        def ln_scalar(gi, stat, src, dst, c_in, c_in_v, waits, inc):
            st = stat[gi % 2]
            sc.wait_ge(c_in, c_in_v)
            for sem, v_ in waits:
                W(sc, "sc", sem, v_)
            i1 = sr([], lambda: sc.activation(out=dst[:], in_=src[:],
                                              func=AF.Square, bias=st[:, 1:2],
                                              scale=1.0, accum_out=st[:, 2:3]))
            i2 = sr([i1], lambda: sc.activation(out=st[:, 3:4], in_=st[:, 2:3],
                                                func=AF.Ln, scale=1.0 / D,
                                                bias=EPS))
            i3 = sr([i2], lambda: sc.activation(out=st[:, 4:5], in_=st[:, 3:4],
                                                func=AF.Exp, scale=-0.5))
            i4 = sr([i3], lambda: sc.mul(st[:, 5:6], st[:, 1:2], st[:, 4:5]))
            sr([i4], lambda: sc.activation(out=dst[:], in_=src[:],
                                           func=AF.Identity,
                                           scale=st[:, 4:5], bias=st[:, 5:6]),
               inc=inc)

        if ST < 2:
            return
        if debug:
            sc.wait_ge(S.d_zcw, 16 * dzw_n[0])
        for t in range(NT):
            ln_scalar(t, stats["s1"], xbuf[t % 2], u_f[t % 2],
                      S.c_v1, t + 1, [(S.c_yassb, 8 * (t - 2) + 8)],
                      (S.c_u, 1))
            if t >= 1:
                ln_scalar(t - 1, stats["s2"], y_sb[(t - 1) % 2], ln2n,
                          S.c_v2, t, [(S.c_x1, t - 1)], (S.c_ln2, 1))
        ln_scalar(NT - 1, stats["s2"], y_sb[(NT - 1) % 2], ln2n,
                  S.c_v2, NT, [(S.c_x1, NT - 1)], (S.c_ln2, 1))

        if stage == 15:
            return
        for t in range(NT):
            ln_scalar(t, stats["s3"], x1r[t % 2], x1nb[t % 2],
                      S.c_v3, t + 1, [(S.c_tr1b, 4 * (t - 2) + 4)],
                      (S.c_ln3, 1))
        if ST < 3:
            return

        for ncx in range(NCH):
            for tg in range(4):
                if tg < 3:
                    sc.wait_ge(S.c_psg, 3 * ncx + tg + 1)
                else:
                    sc.wait_ge(S.c_wfg, ncx + 1)
                W(sc, "sc", S.c_hT, 4 * ncx + tg - 1)
                sr([], lambda tg=tg: sc.activation(out=silu_sb[tg % 2][:],
                                                   in_=pbank(tg)[:, :],
                                                   func=AF.Silu),
                   inc=(S.c_silu, 1))
        if ST < 4:
            return

        for t in range(NT):
            ln_scalar(t, stats["s4"], osb[t % 2], ln4n,
                      S.c_v4, t + 1, [(S.c_fo, t)], (S.c_ln4, 1))

    @blk.vector
    def _(v):
        def vr(reads, inst_f, inc=None):
            for tk in reads:
                if tk:
                    W(v, "v", tk[0], tk[1])
            inst = inst_f()
            if inc is None:
                tick = bump(S.vself)
                inst.then_inc(S.vself, 1)
            else:
                tick = bump(inc[0], inc[1])
                inst.then_inc(inc[0], inc[1])
            return tick

        v.wait_ge(S.pg, 1)
        v.wait_ge(S.d_pro, 16 * 11)

        def trig2(dst_c, dst_s, r_tick, r_src):
            PI_HI = 3.14159274101257324
            PI_ERR = float(np.float64(np.float32(np.pi)) - np.float64(np.pi))
            tvf = vr([r_tick], lambda: v.tensor_scalar(
                out=pB[:], in0=r_src[:], scalar1=6.28318530717958647, scalar2=None,
                op0=ALU.mult))
            tn1 = vr([r_tick], lambda: v.tensor_scalar(
                out=pD[:], in0=r_src[:], scalar1=0.25, scalar2=None, op0=ALU.is_gt))
            tn2 = vr([r_tick], lambda: v.tensor_scalar(
                out=pE[:], in0=r_src[:], scalar1=0.75, scalar2=None, op0=ALU.is_gt))
            tnn = vr([tn1, tn2], lambda: v.tensor_tensor(
                out=pF[:], in0=pD[:], in1=pE[:], op=ALU.add))
            tv1 = vr([tnn, tvf], lambda: v.scalar_tensor_tensor(
                out=pA[:], in0=pF[:], scalar=-PI_HI, in1=pB[:],
                op0=ALU.mult, op1=ALU.add))
            tvv = vr([tv1], lambda: v.scalar_tensor_tensor(
                out=pB[:], in0=pF[:], scalar=PI_ERR, in1=pA[:],
                op0=ALU.mult, op1=ALU.add))
            tsg = vr([tn1, tn2], lambda: v.tensor_tensor(
                out=pC[:], in0=pD[:], in1=pE[:], op=ALU.subtract))
            tsg = vr([tsg], lambda: v.tensor_scalar(
                out=pE[:], in0=pC[:], scalar1=-2.0, scalar2=1.0,
                op0=ALU.mult, op1=ALU.add))
            tv2 = vr([tvv], lambda: v.tensor_tensor(
                out=pA[:], in0=pB[:], in1=pB[:], op=ALU.mult))

            def horner(coefs):
                tk = vr([tv2], lambda: v.tensor_scalar(
                    out=pC[:], in0=pA[:], scalar1=float(coefs[-1]),
                    scalar2=float(coefs[-2]), op0=ALU.mult, op1=ALU.add))
                for c in coefs[-3::-1]:
                    tk2 = vr([tk], lambda: v.tensor_tensor(
                        out=pD[:], in0=pC[:], in1=pA[:], op=ALU.mult))
                    tk = vr([tk2], lambda c=c: v.tensor_scalar(
                        out=pC[:], in0=pD[:], scalar1=float(c), scalar2=None,
                        op0=ALU.add))
                return tk

            tks = horner(SINC)
            tsp = vr([tks, tvv], lambda: v.tensor_tensor(
                out=pD[:], in0=pC[:], in1=pB[:], op=ALU.mult))
            ts_ = vr([tsp, tsg], lambda: v.tensor_tensor(
                out=dst_s[:], in0=pD[:], in1=pE[:], op=ALU.mult))
            tkc = horner(COSC)
            tc_ = vr([tkc], lambda: v.tensor_tensor(
                out=dst_c[:], in0=pC[:], in1=pE[:], op=ALU.mult))
            return tc_, ts_

        gi32 = grid[0:64, 0:64].bitcast(I32)
        gi32b = grid[0:64, 64:128].bitcast(I32)
        tpf = vr([], lambda: v.tensor_scalar(
            out=gi32, in0=colidx[:], scalar1=rowf[:, :], scalar2=None, op0=ALU.mult))
        tpm = vr([tpf], lambda: v.tensor_scalar(
            out=gi32b, in0=gi32, scalar1=63, scalar2=None, op0=ALU.bitwise_and))
        trf = vr([tpm], lambda: v.tensor_copy(out=pF[:], in_=gi32b))
        trf = vr([trf], lambda: v.tensor_scalar(
            out=colf[:], in0=pF[:], scalar1=1.0 / 64, scalar2=None, op0=ALU.mult))
        tc64, ts64 = trig2(c64, s64, trf, colf)
        trf2 = vr([tpf, tc64, ts64], lambda: v.tensor_copy(out=pF[:], in_=gi32))
        trf2 = vr([trf2], lambda: v.tensor_scalar(
            out=colf[:], in0=pF[:], scalar1=1.0 / 4096, scalar2=None, op0=ALU.mult))
        ttwc, ttws = trig2(twc, tws, trf2, colf)
        tcf = vr([ttwc, ttws], lambda: v.tensor_copy(out=pF[:], in_=colidx[:]))
        tcf = vr([tcf], lambda: v.tensor_scalar(
            out=colf[:], in0=pF[:], scalar1=1.0 / 4096, scalar2=None, op0=ALU.mult))
        tcB, tsB = trig2(cosB, sinB, tcf, colf)
        vr([tcB, tsB], lambda: v.memset(toep[:, 0:384], 0.0), inc=(S.pv, 1))

        v.wait_ge(S.ps, 1)
        tiv = vr([], lambda: v.reciprocal(out=pF[0:1, 0:1], in_=step_t[:]))
        vr([tiv], lambda: v.tensor_scalar(
            out=inv2_t[:], in0=pF[0:1, 0:1], scalar1=2.0, scalar2=None,
            op0=ALU.mult), inc=(S.pv, 1))

        v.wait_ge(S.pt, 1)
        te1 = vr([], lambda: v.tensor_copy(out=g2s[:], in_=pbg[0][0:64, 0:1]))
        vr([te1], lambda: v.tensor_copy(out=dcol[:], in_=pbg[1][0:128, 0:1]),
           inc=(S.pv, 1))
        v.wait_ge(S.pt, 2)
        tl1 = vr([], lambda: v.tensor_copy(out=lrep[:, 0:64], in_=pbg[2][0:64, 0:64]))
        tl2 = vr([tl1], lambda: v.tensor_copy(out=lrep[:, 64:128],
                                              in_=pbg[3][0:64, 0:64]),
                 inc=(S.pv, 1))
        v.wait_ge(S.pt, 3)
        tv1 = vr([], lambda: v.tensor_copy(out=vct[:, 0:64], in_=pbg[4][0:64, 0:64]))
        vr([tv1], lambda: v.tensor_copy(out=vct[:, 64:128], in_=pbg[5][0:64, 0:64]),
           inc=(S.pv, 1))
        v.wait_ge(S.pt, 4)
        tb0 = vr([], lambda: v.tensor_copy(out=bcol[:, 0:1], in_=pbg[2][0:64, 0:1]))
        vr([tb0], lambda: v.tensor_copy(out=bcol[:, 1:2], in_=pbg[3][0:64, 0:1]),
           inc=(S.pv, 1))

        cA = c64[:, 1:2]
        sA = s64[:, 1:2]
        t1 = vr([tcB], lambda: v.tensor_scalar(
            out=pA[:], in0=cosB[:], scalar1=cA, scalar2=None, op0=ALU.mult))
        t2 = vr([tsB], lambda: v.tensor_scalar(
            out=pB[:], in0=sinB[:], scalar1=sA, scalar2=None, op0=ALU.mult))
        tom = vr([t1, t2], lambda: v.tensor_tensor(
            out=omc[:], in0=pA[:], in1=pB[:], op=ALU.subtract))
        t3 = vr([], lambda: v.tensor_scalar(
            out=pC[:], in0=cosB[:], scalar1=sA, scalar2=None, op0=ALU.mult))
        t4 = vr([], lambda: v.tensor_scalar(
            out=pD[:], in0=sinB[:], scalar1=cA, scalar2=None, op0=ALU.mult))
        t5 = vr([t3, t4], lambda: v.tensor_tensor(
            out=pE[:], in0=pC[:], in1=pD[:], op=ALU.add))
        tos = vr([t5], lambda: v.tensor_scalar(
            out=oms[:], in0=pE[:], scalar1=-1.0, scalar2=None, op0=ALU.mult))

        tdre = vr([tom], lambda: v.tensor_scalar(
            out=pA[:], in0=omc[:], scalar1=1.0, scalar2=None, op0=ALU.add))
        tq1 = vr([tdre], lambda: v.tensor_tensor(
            out=pB[:], in0=pA[:], in1=pA[:], op=ALU.mult))
        tq2 = vr([tos], lambda: v.tensor_tensor(
            out=pC[:], in0=oms[:], in1=oms[:], op=ALU.mult))
        tq3 = vr([tq1, tq2], lambda: v.tensor_tensor(
            out=pF[:], in0=pB[:], in1=pC[:], op=ALU.add))
        trc = vr([tq3], lambda: v.reciprocal(out=pB[:], in_=pF[:]))
        trdre = vr([trc], lambda: v.tensor_tensor(
            out=pC[:], in0=pA[:], in1=pB[:], op=ALU.mult))
        trdim = vr([trc], lambda: v.tensor_tensor(
            out=pD[:], in0=oms[:], in1=pB[:], op=ALU.mult))
        tcre = vr([trdre], lambda: v.tensor_scalar(
            out=cre[:], in0=pC[:], scalar1=2.0 / 4096, scalar2=None, op0=ALU.mult))
        tcim = vr([trdim], lambda: v.tensor_scalar(
            out=cim[:], in0=pD[:], scalar1=-2.0 / 4096, scalar2=None, op0=ALU.mult))
        tnre = vr([tom], lambda: v.tensor_scalar(
            out=pE[:], in0=omc[:], scalar1=-1.0, scalar2=1.0,
            op0=ALU.mult, op1=ALU.add))
        ta1 = vr([tnre, trdre], lambda: v.tensor_tensor(
            out=pA[:], in0=pE[:], in1=pC[:], op=ALU.mult))
        ta2 = vr([trdim], lambda: v.tensor_tensor(
            out=pB[:], in0=oms[:], in1=pD[:], op=ALU.mult))
        tqre = vr([ta1, ta2], lambda: v.tensor_tensor(
            out=pF[:], in0=pA[:], in1=pB[:], op=ALU.subtract))
        tgre = vr([tqre], lambda: v.tensor_scalar(
            out=gre[:], in0=pF[:], scalar1=g2s[:, :], scalar2=None, op0=ALU.mult))
        tb1 = vr([tnre, trdim], lambda: v.tensor_tensor(
            out=pA[:], in0=pE[:], in1=pD[:], op=ALU.mult))
        tb2 = vr([trdre], lambda: v.tensor_tensor(
            out=pB[:], in0=oms[:], in1=pC[:], op=ALU.mult))
        tqim = vr([tb1, tb2], lambda: v.tensor_tensor(
            out=pF[:], in0=pA[:], in1=pB[:], op=ALU.add))
        tqim = vr([tqim], lambda: v.tensor_scalar(
            out=pE[:], in0=pF[:], scalar1=-1.0, scalar2=None, op0=ALU.mult))
        tgim = vr([tqim], lambda: v.tensor_scalar(
            out=gim[:], in0=pE[:], scalar1=g2s[:, :], scalar2=None, op0=ALU.mult))

        v.wait_ge(S.pt, 5)
        tb1_ = vr([], lambda: v.tensor_copy(out=brow[0:1, 0:64],
                                            in_=pbg[2][0:1, 0:64]))
        tb2_ = vr([tb1_], lambda: v.tensor_copy(out=brow[0:1, 64:128],
                                                in_=pbg[3][0:1, 0:64]))
        ctRe, ctIm = ct_s[0:1, 0:128:2], ct_s[0:1, 1:128:2]
        pRe, pIm = pq_s[0:1, 0:128:2], pq_s[0:1, 1:128:2]
        qRe, qIm = pq_s[0:1, 128:256:2], pq_s[0:1, 129:256:2]
        bRe, bIm = brow[0:1, 0:64], brow[0:1, 64:128]
        row = lambda i: vrow[0:1, i * 64:(i + 1) * 64]
        tmpr = grid[0:1, 0:64]
        tmpi = grid[0:1, 64:128]

        def cmulconj(ar, ai, br, bi, idx, deps):
            ta_ = vr(deps, lambda: v.tensor_tensor(out=tmpr, in0=ar, in1=br,
                                                   op=ALU.mult))
            tb_ = vr(deps, lambda: v.tensor_tensor(out=tmpi, in0=ai, in1=bi,
                                                   op=ALU.mult))
            tre = vr([ta_, tb_], lambda: v.tensor_tensor(
                out=row(2 * idx), in0=tmpr, in1=tmpi, op=ALU.add))
            tc_ = vr([tre], lambda: v.tensor_tensor(out=tmpr, in0=ar, in1=bi,
                                                    op=ALU.mult))
            td_ = vr([tre], lambda: v.tensor_tensor(out=tmpi, in0=ai, in1=br,
                                                    op=ALU.mult))
            return vr([tc_, td_], lambda: v.tensor_tensor(
                out=row(2 * idx + 1), in0=tmpr, in1=tmpi, op=ALU.subtract))

        tva = cmulconj(ctRe, ctIm, bRe, bIm, 0, [tb2_])
        tvb = cmulconj(ctRe, ctIm, pRe, pIm, 1, [tva])
        tvc = cmulconj(qRe, qIm, bRe, bIm, 2, [tvb])
        tvd = cmulconj(qRe, qIm, pRe, pIm, 3, [tvc])
        vr([tvd], lambda: v.tensor_copy(out=pF[0:1, 0:1], in_=vrow[0:1, 0:1]),
           inc=(S.pv, 1))

        v.wait_ge(S.pt, 6)
        tvr_ = vr([], lambda: v.tensor_copy(out=vrep[:], in_=pbg[6][0:64, 0:512]),
                  inc=(S.pv, 1))

        pstep = lrep[:].ap[0][0]
        lre_b = bass.AP(lrep[:].tensor, lrep[:].offset, [[pstep, 64], [0, 64], [1, 64]])
        lim_b = bass.AP(lrep[:].tensor, lrep[:].offset + 64,
                        [[pstep, 64], [0, 64], [1, 64]])
        def g3(t_):
            a = t_[:]
            return a.rearrange("p (a b) -> p a b", a=64)
        gre_b = bass.AP(gre[:].tensor, gre[:].offset,
                        [[gre[:].ap[0][0], 64], [1, 64], [0, 64]])
        gim_b = bass.AP(gim[:].tensor, gim[:].offset,
                        [[gim[:].ap[0][0], 64], [1, 64], [0, 64]])
        tdr = vr([tgre, tl2], lambda: v.tensor_tensor(
            out=g3(dre3), in0=gre_b, in1=lre_b, op=ALU.subtract))
        tdi = vr([tgim], lambda: v.tensor_tensor(
            out=g3(dim3), in0=gim_b, in1=lim_b, op=ALU.subtract))
        tg1 = vr([tdr], lambda: v.tensor_tensor(
            out=grid[0:64, :], in0=dre3[:], in1=dre3[:], op=ALU.mult))
        tg2 = vr([tdi], lambda: v.tensor_tensor(
            out=rre3[:], in0=dim3[:], in1=dim3[:], op=ALU.mult))
        tg3_ = vr([tg1, tg2], lambda: v.tensor_tensor(
            out=rim3[:], in0=grid[0:64, :], in1=rre3[:], op=ALU.add))
        tg4 = vr([tg3_], lambda: v.reciprocal(out=grid[0:64, :], in_=rim3[:]))
        tg5 = vr([tg4], lambda: v.tensor_tensor(
            out=rre3[:], in0=dre3[:], in1=grid[0:64, :], op=ALU.mult))
        tg6 = vr([tg4], lambda: v.tensor_tensor(
            out=rim3[:], in0=dim3[:], in1=grid[0:64, :], op=ALU.mult))

        vrep_b = lambda i: bass.AP(vrep[:].tensor, vrep[:].offset + 64 * i,
                                   [[vrep[:].ap[0][0], 64], [0, 64], [1, 64]])
        kkv = lambda i: kk[:, i * 64:(i + 1) * 64]
        tprev = [tg5, tg6, tvr_]
        for kv in range(4):
            t1_ = vr(tprev, lambda kv=kv: v.tensor_tensor(
                out=g3(dre3), in0=g3(rre3), in1=vrep_b(2 * kv), op=ALU.mult))
            t2_ = vr(tprev, lambda kv=kv: v.tensor_tensor(
                out=g3(dim3), in0=g3(rim3), in1=vrep_b(2 * kv + 1), op=ALU.mult))
            t3a = vr([t1_, t2_], lambda: v.tensor_tensor(
                out=grid[0:64, :], in0=dre3[:], in1=dim3[:], op=ALU.add))
            tre = vr([t3a], lambda kv=kv: v.tensor_reduce(
                out=kkv(2 * kv).unsqueeze(2),
                in_=grid[0:64, :].rearrange("p (a b) -> p a b", a=64),
                axis=AX.X, op=ALU.add))
            t4_ = vr([t3a], lambda kv=kv: v.tensor_tensor(
                out=g3(dre3), in0=g3(rre3), in1=vrep_b(2 * kv + 1), op=ALU.mult))
            t5_ = vr([t3a], lambda kv=kv: v.tensor_tensor(
                out=g3(dim3), in0=g3(rim3), in1=vrep_b(2 * kv), op=ALU.mult))
            t6_ = vr([t4_, t5_, tre], lambda: v.tensor_tensor(
                out=grid[0:64, :], in0=dre3[:], in1=dim3[:], op=ALU.subtract))
            tim = vr([t6_], lambda kv=kv: v.tensor_reduce(
                out=kkv(2 * kv + 1).unsqueeze(2),
                in_=grid[0:64, :].rearrange("p (a b) -> p a b", a=64),
                axis=AX.X, op=ALU.add))
            tprev = [tre, tim]

        k00r, k00i = kkv(0), kkv(1)
        k01r, k01i = kkv(2), kkv(3)
        k10r, k10i = kkv(4), kkv(5)
        k11r, k11i = kkv(6), kkv(7)
        tw1 = vr(tprev, lambda: v.tensor_scalar(
            out=pA[:], in0=k11r, scalar1=1.0, scalar2=None, op0=ALU.add))
        tq1_ = vr([tw1], lambda: v.tensor_tensor(out=pB[:], in0=pA[:], in1=pA[:],
                                                 op=ALU.mult))
        tq2_ = vr(tprev, lambda: v.tensor_tensor(out=pC[:], in0=k11i, in1=k11i,
                                                 op=ALU.mult))
        tq3_ = vr([tq1_, tq2_], lambda: v.tensor_tensor(out=pD[:], in0=pB[:],
                                                        in1=pC[:], op=ALU.add))
        tqr = vr([tq3_], lambda: v.reciprocal(out=pB[:], in_=pD[:]))
        tiwr = vr([tqr], lambda: v.tensor_tensor(out=pC[:], in0=pA[:], in1=pB[:],
                                                 op=ALU.mult))
        tiwi = vr([tqr], lambda: v.tensor_tensor(out=pD[:], in0=k11i, in1=pB[:],
                                                 op=ALU.mult))
        tt1 = vr([tiwr], lambda: v.tensor_tensor(out=pE[:], in0=k01r, in1=k10r,
                                                 op=ALU.mult))
        tt2 = vr([tiwi], lambda: v.tensor_tensor(out=pF[:], in0=k01i, in1=k10i,
                                                 op=ALU.mult))
        ttr = vr([tt1, tt2], lambda: v.tensor_tensor(out=pA[:], in0=pE[:],
                                                     in1=pF[:], op=ALU.subtract))
        tt3 = vr([ttr], lambda: v.tensor_tensor(out=pE[:], in0=k01r, in1=k10i,
                                                op=ALU.mult))
        tt4 = vr([ttr], lambda: v.tensor_tensor(out=pF[:], in0=k01i, in1=k10r,
                                                op=ALU.mult))
        tti = vr([tt3, tt4], lambda: v.tensor_tensor(out=pB[:], in0=pE[:],
                                                     in1=pF[:], op=ALU.add))
        ts1_ = vr([tti], lambda: v.tensor_tensor(out=pE[:], in0=pA[:], in1=pC[:],
                                                 op=ALU.mult))
        ts2_ = vr([tti], lambda: v.tensor_tensor(out=pF[:], in0=pB[:], in1=pD[:],
                                                 op=ALU.mult))
        tsre = vr([ts1_, ts2_], lambda: v.tensor_tensor(
            out=grid[0:64, 0:64], in0=pE[:], in1=pF[:], op=ALU.add))
        ts3_ = vr([tsre], lambda: v.tensor_tensor(out=pE[:], in0=pB[:], in1=pC[:],
                                                  op=ALU.mult))
        ts4_ = vr([tsre], lambda: v.tensor_tensor(out=pF[:], in0=pA[:], in1=pD[:],
                                                  op=ALU.mult))
        tsim = vr([ts3_, ts4_], lambda: v.tensor_tensor(
            out=grid[0:64, 64:128], in0=pE[:], in1=pF[:], op=ALU.subtract))
        trr = vr([tsre], lambda: v.tensor_tensor(
            out=pA[:], in0=k00r, in1=grid[0:64, 0:64], op=ALU.subtract))
        tri = vr([tsim], lambda: v.tensor_tensor(
            out=pB[:], in0=k00i, in1=grid[0:64, 64:128], op=ALU.subtract))
        ta1_ = vr([trr, tcre], lambda: v.tensor_tensor(out=pC[:], in0=cre[:],
                                                       in1=pA[:], op=ALU.mult))
        ta2_ = vr([tri, tcim], lambda: v.tensor_tensor(out=pD[:], in0=cim[:],
                                                       in1=pB[:], op=ALU.mult))
        tare = vr([ta1_, ta2_], lambda: v.tensor_tensor(
            out=aru[:, 0:64], in0=pC[:], in1=pD[:], op=ALU.subtract))
        ta3_ = vr([tare], lambda: v.tensor_tensor(out=pC[:], in0=cre[:], in1=pB[:],
                                                  op=ALU.mult))
        ta4_ = vr([tare], lambda: v.tensor_tensor(out=pD[:], in0=cim[:], in1=pA[:],
                                                  op=ALU.mult))
        vr([ta3_, ta4_], lambda: v.tensor_tensor(
            out=aru[:, 64:128], in0=pC[:], in1=pD[:], op=ALU.add),
           inc=(S.pv, 1))

        v.wait_ge(S.pt, 7)
        tm1 = vr([], lambda: v.tensor_copy(out=m1s[:], in_=pbg[0][0:64, 0:128]))
        tm2 = vr([tm1], lambda: v.tensor_copy(out=m2s[:], in_=pbg[1][0:64, 0:128]))
        ty1 = vr([tm2], lambda: v.tensor_tensor(
            out=yre[:], in0=m1s[:, 0:64], in1=m2s[:, 64:128], op=ALU.subtract))
        ty2 = vr([tm2], lambda: v.tensor_tensor(
            out=yim[:], in0=m1s[:, 64:128], in1=m2s[:, 0:64], op=ALU.add))
        tz1 = vr([ty1], lambda: v.tensor_tensor(out=pA[:], in0=yre[:], in1=twc[:],
                                                op=ALU.mult))
        tz2 = vr([ty2], lambda: v.tensor_tensor(out=pB[:], in0=yim[:], in1=tws[:],
                                                op=ALU.mult))
        tzr = vr([tz1, tz2], lambda: v.tensor_tensor(out=zre[:], in0=pA[:],
                                                     in1=pB[:], op=ALU.subtract))
        tz3 = vr([ty1], lambda: v.tensor_tensor(out=pC[:], in0=yre[:], in1=tws[:],
                                                op=ALU.mult))
        tz4 = vr([ty2], lambda: v.tensor_tensor(out=pD[:], in0=yim[:], in1=twc[:],
                                                op=ALU.mult))
        tzi = vr([tz3, tz4], lambda: v.tensor_tensor(out=zim[:], in0=pC[:],
                                                     in1=pD[:], op=ALU.add))
        vr([tzr, tzi], lambda: v.tensor_copy(out=pF[0:1, 0:1], in_=zre[0:1, 0:1]),
           inc=(S.pv, 1))
        v.wait_ge(S.pt, 8)
        tzt1 = vr([], lambda: v.tensor_copy(out=zret[:], in_=pbg[2][0:64, 0:64]))
        vr([tzt1], lambda: v.tensor_scalar(
            out=zimtn[:], in0=pbg[3][0:64, 0:64], scalar1=-1.0, scalar2=None,
            op0=ALU.mult), inc=(S.pv, 1))
        v.wait_ge(S.pt, 9)
        vr([], lambda: v.tensor_copy(out=ktb[:], in_=pbg[4][0:64, 0:64]),
           inc=(S.pv, 1))

        def unpack_z(src, rows, width, dzw_wait, prev=None):
            zstep = zci[:].ap[0][0]
            tks = []
            for jn in range(8):
                tks.append(vr([prev], lambda jn=jn: v.tensor_scalar(
                    out=bass.AP(zci[:].tensor, zci[:].offset + jn,
                                [[zstep, rows], [8, width]]),
                    in0=src[0:rows, 0:width], scalar1=4 * jn, scalar2=15,
                    op0=ALU.logical_shift_right, op1=ALU.bitwise_and)))
            W(v, "v", S.d_zcw, dzw_wait)
            return vr(tks, lambda: v.tensor_scalar(
                out=zcb[0:rows, 0:width * 8], in0=zci[0:rows, 0:width * 8],
                scalar1=1, scalar2=None, op0=ALU.add), inc=(S.pv, 1))

        tz1 = unpack_z(qzg_s, 32, 1376, 0)
        tz2 = unpack_z(qzu_s, 32, 1376, 16, prev=tz1)
        unpack_z(qzd_s, 86, 512, 32, prev=tz2)
        if ST < 2:
            return

        if debug:
            v.wait_ge(S.d_zcw, 16 * dzw_n[0])
        for t in range(NT + 2):
            if 0 <= t - 2 < NT:
                tm = t - 2
                v.wait_ge(S.c_ln2, tm + 1)
                W(v, "v", [S.d_x1w0, S.d_x1w1][tm % 2], 16 * (tm // 2))
                vr([], lambda tm=tm: v.tensor_tensor(
                    out=x1b[tm % 2][:], in0=xbuf[tm % 2][:], in1=ln2n[:],
                    op=ALU.add), inc=(S.c_x1, 1))
            if t < NT:
                st1 = stats["s1"][t % 2]
                v.wait_ge([S.d_x0, S.d_x1][t % 2], 16 * (t // 2 + 1))
                W(v, "v", S.c_u, t - 1)
                t_sum = vr([], lambda t=t: v.tensor_reduce(
                    out=st1[:, 0:1], in_=xbuf[t % 2][:], axis=AX.X, op=ALU.add))
                vr([t_sum], lambda t=t: v.tensor_scalar(
                    out=st1[:, 1:2], in0=st1[:, 0:1], scalar1=-1.0 / D,
                    scalar2=None, op0=ALU.mult), inc=(S.c_v1, 1))
                v.wait_ge(S.c_u, t + 1)
                W(v, "v", S.c_tr1a, 4 * (t - 2) + 4)
                vr([], lambda t=t: v.tensor_copy(out=ubf[t % 2][:],
                                                 in_=u_f[t % 2][:]),
                   inc=(S.c_ub, 1))
            if 0 <= t - 1 < NT:
                tm = t - 1
                st2 = stats["s2"][tm % 2]
                last_ass = 0
                for h in range(2):
                    v.wait_ge(S.c_conv, 2 * tm + h + 1)
                    for gg in range(4):
                        g_ = 4 * h + gg
                        last_ass = vr([], lambda tm=tm, g_=g_, gg=gg:
                                      v.scalar_tensor_tensor(
                            out=y_sb[tm % 2][:, g_ * 512:(g_ + 1) * 512],
                            in0=u_f[tm % 2][:, g_ * 512:(g_ + 1) * 512],
                            scalar=dcol[:, :], in1=pbank(4 + gg)[:, :],
                            op0=ALU.mult, op1=ALU.add),
                            inc=(S.c_yassb, 1))
                W(v, "v", S.c_ln2, tm - 1)
                t_s2 = vr([last_ass], lambda tm=tm: v.tensor_reduce(
                    out=st2[:, 0:1], in_=y_sb[tm % 2][:], axis=AX.X, op=ALU.add))
                vr([t_s2], lambda tm=tm: v.tensor_scalar(
                    out=st2[:, 1:2], in0=st2[:, 0:1], scalar1=-1.0 / D,
                    scalar2=None, op0=ALU.mult), inc=(S.c_v2, 1))
            if t < NT:
                W(v, "v", S.c_conv, 2 * (t - 2) + 2)
                for grp in range(4):
                    gi = 4 * t + grp
                    v.wait_ge(S.c_tr1a, gi + 1)
                    vr([], lambda t=t, grp=grp, gi=gi: v.tensor_copy(
                        out=uT[t % 2][:, grp * 1024:(grp + 1) * 1024],
                        in_=pbank_bf(gi % 2)[:, 0:1024]),
                       inc=(S.c_uTe, 1))

        if stage == 15:
            return
        v.wait_ge(S.d_x1w0, 16 * NT // 2)
        v.wait_ge(S.d_x1w1, 16 * NT // 2)
        xstep = xt[:].ap[0][0]
        for t in range(NT):
            st3 = stats["s3"][t % 2]
            v.wait_ge([S.d_x1r0, S.d_x1r1][t % 2], 16 * (t // 2 + 1))
            W(v, "v", S.c_ln3, t - 1)
            t_s3 = vr([], lambda t=t: v.tensor_reduce(
                out=st3[:, 0:1], in_=x1r[t % 2][:], axis=AX.X, op=ALU.add))
            vr([t_s3], lambda t=t: v.tensor_scalar(
                out=st3[:, 1:2], in0=st3[:, 0:1], scalar1=-1.0 / D, scalar2=None,
                op0=ALU.mult), inc=(S.c_v3, 1))
            for grp in range(4):
                gi = 4 * t + grp
                v.wait_ge(S.c_tr1b, gi + 1)
                vr([], lambda t=t, grp=grp, gi=gi: v.tensor_copy(
                    out=bass.AP(xt[:].tensor,
                                xt[:].offset + (8 * grp) * TOK + t * P,
                                [[xstep, P], [TOK, 8], [1, P]]),
                    in_=pbank_bf(gi % 2)[:, 0:1024]),
                   inc=(S.c_xTe, 1))
        if ST < 3:
            return

        def deq_gu(m, ncx, c_wf, c_dq, qfree):
            W(v, "v", c_wf, ncx)
            v.wait_ge([S.d_qw0, S.d_qw1][ncx % 2], 16 * 2 * (ncx // 2 + 1))
            v.wait_ge([S.d_azc0, S.d_azc1][ncx % 2], 64 * (ncx // 2 + 1))
            last = {}
            for j in range(8):
                tsh = vr([], lambda m=m, ncx=ncx, j=j: v.tensor_scalar(
                    out=w4i[j % 2][:], in0=qwb[m][ncx % 2][:], scalar1=4 * j,
                    scalar2=15, op0=ALU.logical_shift_right, op1=ALU.bitwise_and))
                tsub = vr([tsh, last.get(("f", j % 2), 0)],
                          lambda m=m, ncx=ncx, j=j: v.scalar_tensor_tensor(
                    out=wfb[j % 2][:], in0=w4i[j % 2][:], scalar=1.0,
                    in1=zcbuf[m][ncx % 2][:], op0=ALU.mult, op1=ALU.subtract))
                vr([tsub], lambda m=m, ncx=ncx, j=j: v.tensor_tensor(
                    out=wtil[m][:, j * 512:(j + 1) * 512], in0=wfb[j % 2][:],
                    in1=ab[m][ncx % 2][:], op=ALU.mult),
                   inc=((c_dq, 1) if j == 7 else None))
                last[("f", j % 2)] = tsub

        def deq_wd(u_i):
            kb, o = WD_UNITS[u_i]
            Kp = 128 if kb < 10 else 96
            v.wait_ge([S.d_qd0, S.d_qd1][u_i % 2], 16 * 3 * (u_i // 2 + 1))
            tprevf = 0
            for j in range(8):
                W(v, "v", [S.d_wdw0, S.d_wdw1][j % 2],
                  16 * (u_i * 4 + j // 2))
                tsh = vr([tprevf], lambda u_i=u_i, j=j, Kp=Kp: v.tensor_scalar(
                    out=w4d[0:Kp, :], in0=qwdb[u_i % 2][0:Kp, :], scalar1=4 * j,
                    scalar2=15, op0=ALU.logical_shift_right, op1=ALU.bitwise_and))
                tsub = vr([tsh], lambda u_i=u_i, Kp=Kp: v.scalar_tensor_tensor(
                    out=wfd[0:Kp, :], in0=w4d[0:Kp, :], scalar=1.0,
                    in1=zcdb[u_i % 2][0:Kp, :], op0=ALU.mult, op1=ALU.subtract))
                vr([tsub], lambda u_i=u_i, j=j, Kp=Kp: v.tensor_tensor(
                    out=wdo[j % 2][0:Kp, :], in0=wfd[0:Kp, :],
                    in1=adb[u_i % 2][0:Kp, :], op=ALU.mult),
                   inc=(S.c_dqd, 1))
                tprevf = tsub

        def ht_evac(ncx):
            for tg in range(4):
                if tg < 3:
                    v.wait_ge(S.c_psu, 3 * ncx + tg + 1)
                else:
                    v.wait_ge(S.c_wfu, ncx + 1)
                v.wait_ge(S.c_silu, 4 * ncx + tg + 1)
                W(v, "v", [S.d_hw0, S.d_hw1][tg % 2],
                  16 * (2 * ncx + tg // 2))
                vr([], lambda ncx=ncx, tg=tg: v.tensor_tensor(
                    out=hTb[tg % 2][:], in0=silu_sb[tg % 2][:],
                    in1=pbank(4 + tg)[:, :], op=ALU.mult),
                   inc=(S.c_hT, 1))

        for ncx in range(NCH):
            deq_gu("g", ncx, S.c_wfg, S.c_dqg, False)
            if ncx > 0:
                ht_evac(ncx - 1)
            deq_gu("u", ncx, S.c_wfu, S.c_dqu, True)
            if ncx < NWU:
                deq_wd(ncx)
        deq_wd(NWU - 2)
        deq_wd(NWU - 1)
        ht_evac(NCH - 1)
        if ST < 4:
            return

        for b in range(2):
            for mg in range(8):
                for tt in range(8):
                    idx = (b * 8 + mg) * 8 + tt
                    v.wait_ge(S.c_ps3, idx + 1)
                    W(v, "v", [S.d_slab0, S.d_slab1][idx % 2], 16 * (idx // 2))
                    vr([], lambda tt=tt, idx=idx: v.tensor_copy(
                        out=slab[idx % 2][:], in_=pbank(tt)[:, :]),
                       inc=(S.c_slabv, 1))

        for t in range(NT):
            st4 = stats["s4"][t % 2]
            v.wait_ge([S.d_osr0, S.d_osr1][t % 2], 32 * (t // 2 + 1))
            W(v, "v", S.c_ln4, t - 1)
            t_s4 = vr([], lambda t=t: v.tensor_reduce(
                out=st4[:, 0:1], in_=osb[t % 2][:], axis=AX.X, op=ALU.add))
            vr([t_s4], lambda t=t: v.tensor_scalar(
                out=st4[:, 1:2], in0=st4[:, 0:1], scalar1=-1.0 / D, scalar2=None,
                op0=ALU.mult), inc=(S.c_v4, 1))
            v.wait_ge(S.c_ln4, t + 1)
            W(v, "v", [S.d_out0, S.d_out1][t % 2], 16 * (t // 2))
            vr([], lambda t=t: v.tensor_tensor(
                out=fout[t % 2][:], in0=x1c[t % 2][:], in1=ln4n[:], op=ALU.add),
               inc=(S.c_fo, 1))

    @blk.tensor
    def _(te):
        te.wait_ge(S.pg, 1)
        te.wait_ge(S.d_pro, 16 * 11)
        te.wait_ge(S.pv, 2)
        te.matmul(pbg[0][0:64, 0:1], onesr[0:1, 0:64], inv2_t[:],
                  start=True, stop=True)
        te.matmul(pbg[1][0:128, 0:1], onesr[0:1, :], dls_s[0:1, 0:1],
                  start=True, stop=True).then_inc(S.pt, 1)
        te.matmul(pbg[2][0:64, 0:64], onesr[0:1, 0:64], lam_s[0:1, 0:128:2],
                  start=True, stop=True)
        te.matmul(pbg[3][0:64, 0:64], onesr[0:1, 0:64], lam_s[0:1, 1:128:2],
                  start=True, stop=True).then_inc(S.pt, 1)
        te.transpose(pbg[4][0:64, 0:64], vc_s[:, 0:128:2], ident_f[0:64, 0:64])
        te.transpose(pbg[5][0:64, 0:64], vc_s[:, 1:128:2],
                     ident_f[0:64, 0:64]).then_inc(S.pt, 1)
        te.wait_ge(S.pv, 5)
        te.matmul(pbg[2][0:64, 0:1], vct[:, 0:64], bp_s[:], start=True, stop=True)
        te.matmul(pbg[3][0:64, 0:1], vct[:, 64:128], bp_s[:],
                  start=True, stop=True).then_inc(S.pt, 1)
        te.wait_ge(S.pv, 6)
        te.transpose(pbg[2][0:1, 0:64], bcol[:, 0:1], ident_f[0:64, 0:64])
        te.transpose(pbg[3][0:1, 0:64], bcol[:, 1:2],
                     ident_f[0:64, 0:64]).then_inc(S.pt, 1)
        te.wait_ge(S.pv, 7)
        te.matmul(pbg[6][0:64, 0:512], onesr[0:1, 0:64], vrow[0:1, :],
                  start=True, stop=True).then_inc(S.pt, 1)
        te.wait_ge(S.pv, 9)
        te.matmul(pbg[0][0:64, 0:128], c64[:], aru[:], start=True, stop=True)
        te.matmul(pbg[1][0:64, 0:128], s64[:], aru[:],
                  start=True, stop=True).then_inc(S.pt, 1)
        te.wait_ge(S.pv, 10)
        te.transpose(pbg[2][0:64, 0:64], zre[:], ident_f[0:64, 0:64])
        te.transpose(pbg[3][0:64, 0:64], zim[:],
                     ident_f[0:64, 0:64]).then_inc(S.pt, 1)
        te.wait_ge(S.pv, 11)
        te.matmul(pbg[4][0:64, 0:64], c64[:], zret[:], start=True, stop=False)
        te.matmul(pbg[4][0:64, 0:64], s64[:], zimtn[:],
                  start=False, stop=True).then_inc(S.pt, 1)
        if ST < 2:
            return

        te.wait_ge(S.pv, 12)
        if debug:
            te.wait_ge(S.d_zcw, 16 * dzw_n[0])
        for t in range(NT + 1):
            if 0 <= t - 1 < NT:
                tm = t - 1
                te.wait_ge(S.c_uTe, 4 * tm + 4)
                if tm == 0:
                    te.wait_ge(S.d_toep, 16 * 32)
                for h in range(2):
                    gs = [4 * h + gg for gg in range(4)]
                    for gg in range(4):
                        W(te, "te", S.c_yassb, 8 * tm + 4 * h + gg - 3)
                    kmax = 4 * gs[-1] + 3
                    for kcx in range(kmax + 1):
                        for gg, g_ in enumerate(gs):
                            if kcx > 4 * g_ + 3:
                                continue
                            inst = te.matmul(
                                pbank(4 + gg)[:, :],
                                uT[tm % 2][:, kcx * 128:(kcx + 1) * 128],
                                toep[:, (3 + 4 * g_ - kcx) * 128:
                                     (3 + 4 * g_ - kcx) * 128 + 512],
                                start=(kcx == 0), stop=(kcx == 4 * g_ + 3))
                            if kcx == kmax and g_ == gs[-1]:
                                inst.then_inc(S.c_conv, 1)
            if t < NT:
                te.wait_ge(S.c_ub, t + 1)
                for grp in range(4):
                    gi = 4 * t + grp
                    W(te, "te", S.c_uTe, gi - 1)
                    for c in range(8):
                        kcx = 8 * grp + c
                        inst = te.transpose(
                            pbank_bf(gi % 2)[:, c * 128:(c + 1) * 128],
                            ubf[t % 2][:, kcx * 128:(kcx + 1) * 128], ident_b[:])
                        if c == 7:
                            inst.then_inc(S.c_tr1a, 1)
        for t in range(NT):
            te.wait_ge(S.c_ln3, t + 1)
            for grp in range(4):
                gi = 4 * t + grp
                W(te, "te", S.c_xTe, gi - 1)
                for c in range(8):
                    cp_ = 8 * grp + c
                    jj, bb = cp_ >> 2, cp_ & 3
                    src = x1nb[t % 2][:, 1024 * bb + jj: 1024 * bb + jj + 1017: 8]
                    inst = te.transpose(
                        pbank_bf(gi % 2)[:, c * 128:(c + 1) * 128], src, ident_b[:])
                    if c == 7:
                        inst.then_inc(S.c_tr1b, 1)
        if ST < 3:
            return

        te.wait_ge(S.c_xTe, 4 * NT)
        te.wait_ge(S.c_yassb, 8 * NT)
        te.wait_ge(S.c_uTe, 4 * NT)
        for ncx in range(NCH):
            for m, c_dq, c_wf, c_free, c_ps, bb in (
                    ("g", S.c_dqg, S.c_wfg, S.c_silu, S.c_psg, 0),
                    ("u", S.c_dqu, S.c_wfu, S.c_hT, S.c_psu, 4)):
                te.wait_ge(c_dq, ncx + 1)
                for tg in range(4):
                    W(te, "te", c_free, 4 * (ncx - 1) + tg + 1)
                for cp_ in range(KC):
                    for tg in range(4):
                        inst = te.matmul(
                            pbank(bb + tg)[:, :],
                            wtil[m][:, cp_ * 128:(cp_ + 1) * 128],
                            xt[:, cp_ * TOK + tg * 512: cp_ * TOK + tg * 512 + 512],
                            start=(cp_ == 0), stop=(cp_ == KC - 1))
                        if cp_ == KC - 1:
                            inst.then_inc(c_ps if tg < 3 else c_wf, 1)
        if ST < 4:
            return

        te.wait_ge(S.c_silu, 4 * NCH)
        te.wait_ge(S.c_hT, 4 * NCH)
        for b in range(2):
            for mg in range(8):
                for grp in range(NGRP):
                    gidx = (b * 8 + mg) * NGRP + grp
                    te.wait_ge([S.d_wdr0, S.d_wdr1][gidx % 2],
                               16 * (gidx // 2 + 1))
                    for ci in range(4):
                        cd = grp * 4 + ci
                        kb, j, Kp = DCH[cd]
                        if mg == 0:
                            for q_ in range(min(cd // 11, 7) + 1):
                                W(te, "te", getattr(S, f"d_hr{q_}"),
                                  16 * 11 * (b + 1))
                        for tt in range(8):
                            if cd == 0:
                                W(te, "te", S.c_slabv,
                                  (b * 8 + mg - 1) * 8 + tt + 1)
                            inst = te.matmul(
                                pbank(tt)[:, :],
                                htblk[0:Kp, cd * 1024 + tt * 128:
                                      cd * 1024 + tt * 128 + 128],
                                wdbuf[grp % 2][0:Kp, ci * 512:(ci + 1) * 512],
                                start=(cd == 0), stop=(cd == NDCH - 1))
                            if cd == NDCH - 1:
                                inst.then_inc(S.c_ps3, 1)
                            elif ci == 3 and tt == 7:
                                inst.then_inc(S.c_mm3, 1)

    blk.__exit__(None, None, None)
    return nc


_NC_CACHE = {}


def kernel(**inputs):
    n = 8
    x = np.ascontiguousarray(np.asarray(inputs["L0_attn_u"], np.float32)
                             .reshape(16384, D))
    shards = np.split(x, n, axis=0)
    if 4 not in _NC_CACHE:
        _NC_CACHE[4] = build_program(stage=4, debug=False)
    nc = _NC_CACHE[4]
    base = {k: np.ascontiguousarray(np.asarray(inputs[k]).reshape(s), d)
            for k, s, d in [
                ("L0_attn_Lambda_ri", (64, 2), np.float32),
                ("L0_attn_p_ri", (64, 2), np.float32),
                ("L0_attn_q_ri", (64, 2), np.float32),
                ("L0_attn_Vc_ri", (64, 128), np.float32),
                ("L0_attn_Ct_ri", (1, 128), np.float32),
                ("L0_attn_Bp", (64, 1), np.float32),
                ("L0_attn_D", (1,), np.float32),
                ("L0_attn_log_step", (1,), np.float32),
                ("L0_ffn_gate_qweight", (512, INTER), np.int32),
                ("L0_ffn_gate_scales", (32, INTER), np.float32),
                ("L0_ffn_gate_qzeros", (32, 1376), np.int32),
                ("L0_ffn_gate_g_idx", (D,), np.int32),
                ("L0_ffn_up_qweight", (512, INTER), np.int32),
                ("L0_ffn_up_scales", (32, INTER), np.float32),
                ("L0_ffn_up_qzeros", (32, 1376), np.int32),
                ("L0_ffn_up_g_idx", (D,), np.int32),
                ("L0_ffn_down_qweight", (1376, D), np.int32),
                ("L0_ffn_down_scales", (86, D), np.float32),
                ("L0_ffn_down_qzeros", (86, 512), np.int32),
                ("L0_ffn_down_g_idx", (INTER,), np.int32),
            ]}
    in_maps = [dict(base, L0_attn_u=shards[i]) for i in range(n)]
    res = run_bass_kernel_spmd(nc, in_maps, core_ids=list(range(n)))
    outs = [np.asarray(r["out"]) for r in res.results]
    return np.concatenate(outs, axis=0).reshape(4, 4096, 4096)


# version 08
# speedup vs baseline: 1.1382x; 1.1382x over previous
import numpy as np
import ml_dtypes
import concourse.bass as bass
import concourse.mybir as mybir
from concourse.bass_utils import run_bass_kernel_spmd

F32 = mybir.dt.float32
BF16 = mybir.dt.bfloat16
I32 = mybir.dt.int32
AF = mybir.ActivationFunctionType
ALU = mybir.AluOpType
AX = mybir.AxisListType

P = 128
D = 4096
TOK = 2048
NT = TOK // P
KC = D // P
INTER = 11008
NCH = INTER // P
EPS = 1e-5
PI = float(np.pi)

DCH = []
for _kb in range(11):
    for _j in range(8):
        DCH.append((_kb, _j, 128 if _kb < 10 else 96))
NDCH = len(DCH)
NGRP = NDCH // 4
WD_UNITS = [(kb, o) for kb in range(11) for o in range(8)]
NWU = len(WD_UNITS)

SINC = [1.0, -1.0 / 6, 1.0 / 120, -1.0 / 5040, 1.0 / 362880, -1.0 / 39916800,
        1.0 / 6227020800.0]
COSC = [1.0, -0.5, 1.0 / 24, -1.0 / 720, 1.0 / 40320, -1.0 / 3628800,
        1.0 / 479001600.0]


def build_program(stage=4, debug=False):
    ST = 2 if stage == 15 else stage
    nc = bass.Bass("TRN2", target_bir_lowering=False)

    def dram_in(name, shape, dt):
        return nc.dram_tensor(name, list(shape), dt, kind="ExternalInput")

    x_h = dram_in("L0_attn_u", [TOK, D], F32)
    lam_h = dram_in("L0_attn_Lambda_ri", [64, 2], F32)
    p_h = dram_in("L0_attn_p_ri", [64, 2], F32)
    q_h = dram_in("L0_attn_q_ri", [64, 2], F32)
    vc_h = dram_in("L0_attn_Vc_ri", [64, 128], F32)
    ct_h = dram_in("L0_attn_Ct_ri", [1, 128], F32)
    bp_h = dram_in("L0_attn_Bp", [64, 1], F32)
    d_h = dram_in("L0_attn_D", [1], F32)
    ls_h = dram_in("L0_attn_log_step", [1], F32)
    qwg_h = dram_in("L0_ffn_gate_qweight", [512, INTER], I32)
    sg_h = dram_in("L0_ffn_gate_scales", [32, INTER], F32)
    zg_h = dram_in("L0_ffn_gate_qzeros", [32, 1376], I32)
    qwu_h = dram_in("L0_ffn_up_qweight", [512, INTER], I32)
    su_h = dram_in("L0_ffn_up_scales", [32, INTER], F32)
    zu_h = dram_in("L0_ffn_up_qzeros", [32, 1376], I32)
    qwd_h = dram_in("L0_ffn_down_qweight", [1376, D], I32)
    sd_h = dram_in("L0_ffn_down_scales", [86, D], F32)
    zd_h = dram_in("L0_ffn_down_qzeros", [86, 512], I32)
    dram_in("L0_ffn_gate_g_idx", [D], I32)
    dram_in("L0_ffn_up_g_idx", [D], I32)
    dram_in("L0_ffn_down_g_idx", [INTER], I32)

    out_h = nc.dram_tensor("out", [TOK, D], F32, kind="ExternalOutput")
    sk = "ExternalOutput" if debug else "Internal"
    x1_h = nc.dram_tensor("x1_scr", [TOK, D], F32, kind=sk)
    kpad_h = nc.dram_tensor("kpad_scr", [4224], BF16, kind=sk)
    kst_h = nc.dram_tensor("kstage_scr", [4353 * 127 + 4224], BF16, kind="Internal")
    zcg_h = nc.dram_tensor("zcg_scr", [32, INTER], BF16, kind="Internal")
    zcu_h = nc.dram_tensor("zcu_scr", [32, INTER], BF16, kind="Internal")
    zcd_h = nc.dram_tensor("zcd_scr", [86, D], BF16, kind="Internal")
    hT_h = nc.dram_tensor("hT_scr", [INTER, TOK], BF16, kind=sk)
    wd_h = nc.dram_tensor("wd_scr", [8 * NDCH * P * 512], BF16, kind="Internal")
    arep = {"g": nc.dram_tensor("arep_g", [512, INTER], F32, kind="Internal"),
            "u": nc.dram_tensor("arep_u", [512, INTER], F32, kind="Internal")}
    zrep = {"g": nc.dram_tensor("zrep_g", [512, INTER], BF16, kind="Internal"),
            "u": nc.dram_tensor("zrep_u", [512, INTER], BF16, kind="Internal")}
    osc_h = nc.dram_tensor("osc_scr", [TOK, D], F32, kind=sk)
    dbg_h = nc.dram_tensor("dbg_scr", [64, 64 * 52], F32, kind=sk) if debug else None

    base = (nc.sbuf_base + 31) // 32 * 32
    top = nc.sbuf_top

    def sba(name, shape, dt, at):
        off = base + at
        nbytes = int(np.prod(shape[1:])) * mybir.dt.size(dt)
        assert off + nbytes <= top, (name, off + nbytes - top)
        return nc.alloc_sbuf_tensor_at(name, list(shape), dt, offset=off)

    acur = [0]
    def sbA(name, shape, dt):
        nbytes = int(np.prod(shape[1:])) * mybir.dt.size(dt)
        off = (acur[0] + 31) // 32 * 32
        acur[0] = off + nbytes
        return sba(name, shape, dt, off)

    ident_f = sbA("ident_f", [P, P], F32)
    ident_b = sbA("ident_b", [P, P], BF16)
    ones_f = sbA("ones_f", [P, P], F32)
    ones_b = sbA("ones_b", [P, P], BF16)
    onesr = sbA("onesr", [1, P], F32)
    dcol = sbA("dcol", [P, 1], F32)
    g2s = sbA("g2s", [64, 1], F32)
    step_t = sbA("step_t", [1, 1], F32)
    inv2_t = sbA("inv2_t", [1, 1], F32)
    kpz = sbA("kpz", [1, 128], BF16)
    bcol = sbA("bcol", [64, 2], F32)
    cst = sbA("cst", [P, 3], F32)
    toep = sbA("toep", [P, 35 * 128], BF16)
    stats = {nm: [sbA(f"st_{nm}_{i}", [P, 6], F32) for i in range(2)]
             for nm in ("s1", "s2", "s3", "s4")}
    assert acur[0] < 13504, acur[0]

    B0 = 13504
    xbuf = [sba(f"xbuf{i}", [P, D], F32, B0 + i * 16384) for i in range(2)]
    xbuf.append(sba("xbuf2", [P, D], F32, B0 + 180224))
    u_f = [sba(f"u_f{i}", [P, D], F32, B0 + 32768 + i * 16384) for i in range(2)]
    ubf = [sba(f"ubf{i}", [P, D], BF16, B0 + 65536 + i * 8192) for i in range(2)]
    uT = [sba(f"uT{i}", [P, D], BF16, B0 + 81920 + i * 8192) for i in range(2)]
    y_sb = [sba(f"y_sb{i}", [P, D], F32, B0 + 98304 + i * 16384) for i in range(2)]
    ln2n = sba("ln2n", [P, D], F32, B0 + 131072)
    x1b = [sba(f"x1b{i}", [P, D], F32, B0 + 147456 + i * 16384) for i in range(2)]

    pcur = [B0 + 32768]
    def sbP(name, shape, dt):
        nbytes = int(np.prod(shape[1:])) * mybir.dt.size(dt)
        off = (pcur[0] + 31) // 32 * 32
        pcur[0] = off + nbytes
        return sba(name, shape, dt, off)

    c64 = sbP("c64", [64, 64], F32)
    s64 = sbP("s64", [64, 64], F32)
    twc = sbP("twc", [64, 64], F32)
    tws = sbP("tws", [64, 64], F32)
    cosB = sbP("cosB", [64, 64], F32)
    sinB = sbP("sinB", [64, 64], F32)
    colidx = sbP("colidx", [64, 64], I32)
    rowf = sbP("rowf", [64, 1], F32)
    colf = sbP("colf", [64, 64], F32)
    pA = sbP("pA", [64, 64], F32)
    pB = sbP("pB", [64, 64], F32)
    pC = sbP("pC", [64, 64], F32)
    pD = sbP("pD", [64, 64], F32)
    pE = sbP("pE", [64, 64], F32)
    pF = sbP("pF", [64, 64], F32)
    omc = sbP("omc", [64, 64], F32)
    oms = sbP("oms", [64, 64], F32)
    gre = sbP("gre", [64, 64], F32)
    gim = sbP("gim", [64, 64], F32)
    cre = sbP("cre", [64, 64], F32)
    cim = sbP("cim", [64, 64], F32)
    lam_s = sbP("lam_s", [1, 128], F32)
    pq_s = sbP("pq_s", [1, 256], F32)
    vc_s = sbP("vc_s", [64, 128], F32)
    ct_s = sbP("ct_s", [1, 128], F32)
    bp_s = sbP("bp_s", [64, 1], F32)
    dls_s = sbP("dls_s", [1, 2], F32)
    vct = sbP("vct", [64, 128], F32)
    brow = sbP("brow", [1, 128], F32)
    vrow = sbP("vrow", [1, 512], F32)
    vrep = sbP("vrep", [64, 512], F32)
    lrep = sbP("lrep", [64, 128], F32)
    kk = sbP("kk", [64, 512], F32)
    aru = sbP("aru", [64, 128], F32)
    m1s = sbP("m1s", [64, 128], F32)
    m2s = sbP("m2s", [64, 128], F32)
    yre = sbP("yre", [64, 64], F32)
    yim = sbP("yim", [64, 64], F32)
    zre = sbP("zre", [64, 64], F32)
    zim = sbP("zim", [64, 64], F32)
    zret = sbP("zret", [64, 64], F32)
    zimtn = sbP("zimtn", [64, 64], F32)
    ktb = sbP("ktb", [64, 64], BF16)
    dre3_off = (pcur[0] + 31) // 32 * 32 - 0
    dre3 = sbP("dre3", [64, 4096], F32)
    dim3 = sbP("dim3", [64, 4096], F32)
    rre3 = sbP("rre3", [64, 4096], F32)
    rim3 = sbP("rim3", [64, 4096], F32)
    grid = sbP("grid", [64, 4096], F32)
    qzg_s = sbP("qzg_s", [32, 1376], I32)
    qzu_s = sbP("qzu_s", [32, 1376], I32)
    qzd_s = sbP("qzd_s", [86, 512], I32)
    zci = sba("zci", [86, INTER], I32, dre3_off)
    zcb = sba("zcb", [86, INTER], BF16, dre3_off + 44032)
    assert pcur[0] < B0 + 180224, pcur[0]

    x1r = [sba(f"x1r{i}", [P, D], F32, B0 + i * 16384) for i in range(2)]
    x1nb = [sba(f"x1nb{i}", [P, D], BF16, B0 + 32768 + i * 8192) for i in range(2)]
    XT0 = 81728
    xt = sba("xt", [P, KC * TOK], BF16, XT0)

    dcur = [B0]
    def sbD(name, shape, dt):
        nbytes = int(np.prod(shape[1:])) * mybir.dt.size(dt)
        off = (dcur[0] + 31) // 32 * 32
        dcur[0] = off + nbytes
        return sba(name, shape, dt, off)

    qwb = {m: [sbD(f"qwb{m}{i}", [P, 512], I32) for i in range(2)] for m in "gu"}
    ab = {m: [sbD(f"ab{m}{i}", [P, 512], F32) for i in range(2)] for m in "gu"}
    zcbuf = {m: [sbD(f"zcbf{m}{i}", [P, 512], BF16) for i in range(2)] for m in "gu"}
    w4i = [sbD(f"w4i{i}", [P, 512], I32) for i in range(2)]
    wfb = [sbD(f"wfb{i}", [P, 512], F32) for i in range(2)]
    wtil = {m: sbD(f"wtil{m}", [P, D], BF16) for m in "gu"}
    silu_sb = [sbD(f"silu{i}", [P, 512], F32) for i in range(2)]
    hTb = [sbD(f"hTb{i}", [P, 512], BF16) for i in range(2)]
    qwdb = [sbD(f"qwdb{i}", [P, 512], I32) for i in range(2)]
    adb = [sbD(f"adb{i}", [P, 512], F32) for i in range(2)]
    zcdb = [sbD(f"zcdb{i}", [P, 512], BF16) for i in range(2)]
    w4d = sbD("w4d", [P, 512], I32)
    wfd = sbD("wfd", [P, 512], F32)
    wdo = [sbD(f"wdo{i}", [P, 512], BF16) for i in range(2)]
    assert dcur[0] <= XT0, dcur[0]

    htblk = sba("htblk", [P, NDCH * 1024], BF16, B0)
    E1 = B0 + NDCH * 1024 * 2
    wdbuf = [sba(f"wdbuf{i}", [P, 2048], BF16, E1 + i * 4096) for i in range(2)]
    slab = [sba(f"slab{i}", [P, 512], F32, E1 + 8192 + i * 2048) for i in range(2)]

    osb = [sba(f"osb{i}", [P, D], F32, B0 + i * 16384) for i in range(2)]
    x1c = [sba(f"x1c{i}", [P, D], F32, B0 + 32768 + i * 16384) for i in range(2)]
    ln4n = sba("ln4n", [P, D], F32, B0 + 65536)
    fout = [sba(f"fout{i}", [P, D], F32, B0 + 81920 + i * 16384) for i in range(2)]

    pbg = [nc.psum_tensor(f"pb{i}", [P, 512], F32).__enter__() for i in range(8)]
    def pbank(i):
        return pbg[i][:]
    def pbank_bf(i):
        return pbg[i][:].bitcast(BF16)

    sems = {}
    for nm in ["d_pro", "d_x0", "d_x1", "d_x2", "d_x1r0", "d_x1r1", "d_qw0", "d_qw1",
               "d_azc0", "d_azc1", "d_rep", "d_qd0", "d_qd1", "d_toep", "d_kpad", "d_kst",
               "d_zcw", "d_x1w0", "d_x1w1", "d_hw0", "d_hw1", "d_wdw0", "d_wdw1",
               "d_hr0", "d_hr1", "d_hr2", "d_hr3", "d_hr4", "d_hr5", "d_hr6", "d_hr7", "d_wdr0", "d_wdr1", "d_slab0", "d_slab1",
               "d_osr0", "d_osr1", "d_out0", "d_out1", "pg", "ps", "pv", "pt",
               "c_v1", "c_u", "c_ub", "c_tr1a", "c_uTe", "c_conv", "c_yassb",
               "c_v2", "c_ln2", "c_x1", "c_v3", "c_ln3", "c_tr1b", "c_xTe",
               "c_dqg", "c_dqu", "c_wfg", "c_wfu", "c_psg", "c_psu",
               "c_silu", "c_hT", "c_qfree", "c_dqd",
               "c_mm3", "c_ps3", "c_slabv", "c_v4", "c_ln4", "c_fo",
               "vself", "sself", "gself"]:
        sems[nm] = nc.alloc_semaphore(name=nm)
    S = type("S", (), sems)

    nc.const_aps.register(F32, EPS, cst[:, 0:1])
    nc.const_aps.register(F32, 1.0 / D, cst[:, 1:2])
    nc.const_aps.register(F32, -0.5, cst[:, 2:3])

    hwm = {}
    def W(eng, tag, sem, val):
        if val <= 0:
            return
        key = (tag, sem.name)
        if hwm.get(key, -1) >= val:
            return
        hwm[key] = val
        eng.wait_ge(sem, val)

    counts = {}
    def bump(sem, amt=1):
        counts[sem.name] = counts.get(sem.name, 0) + amt
        return (sem, counts[sem.name])

    dzw_n = [3]
    blk = nc.Block().__enter__()

    @blk.gpsimd
    def _(g):
        g.iota(colidx[:], [[1, 64]], channel_multiplier=0)
        g.iota(rowf[:], [[0, 1]], channel_multiplier=1,
               allow_small_or_imprecise_dtypes=True)
        g.memset(ones_f[:], 1.0)
        g.memset(ones_b[:], 1.0)
        g.memset(onesr[:], 1.0)
        g.memset(kpz[:], 0.0)
        g.memset(cst[:, 1:2], 1.0 / D)
        g.memset(cst[:, 2:3], -0.5)
        g.memset(cst[:, 0:1], EPS).then_inc(S.gself, 1)
        g.wait_ge(S.gself, 1)
        g.affine_select(out=ident_f[:], in_=ones_f[:], pattern=[[-1, 128]],
                        compare_op=ALU.is_equal, fill=0.0, base=0,
                        channel_multiplier=1)
        g.affine_select(out=ident_b[:], in_=ones_b[:], pattern=[[-1, 128]],
                        compare_op=ALU.is_equal, fill=0.0, base=0,
                        channel_multiplier=1).then_inc(S.pg, 1)

        g.wait_ge(S.pv, 12)
        g.dma_start(out=bass.AP(kpad_h, 0, [[1, 128]]),
                    in_=kpz[0:1, :]).then_inc(S.d_kpad, 16)
        g.dma_start(out=bass.AP(kpad_h, 128, [[64, 64], [1, 64]]),
                    in_=ktb[:]).then_inc(S.d_kpad, 16)
        g.wait_ge(S.d_kpad, 32)
        for pp in range(128):
            g.dma_start(out=bass.AP(kst_h, 4353 * pp, [[1, 4224]]),
                        in_=bass.AP(kpad_h, 0, [[1, 4224]])).then_inc(S.d_kst, 16)
        for m, sh in (("g", sg_h), ("u", su_h)):
            for bb in range(4):
                g.dma_start(out=bass.AP(arep[m], bb * 128 * INTER,
                                        [[INTER, 128], [1, INTER]]),
                            in_=bass.AP(sh, 8 * bb * INTER,
                                        [[INTER, 8], [0, 16], [1, INTER]])
                            ).then_inc(S.d_rep, 16)
        g.wait_ge(S.pv, 13)
        g.dma_start(out=zcg_h.ap(), in_=zcb[0:32, :]).then_inc(S.d_zcw, 16)
        g.wait_ge(S.pv, 14)
        g.dma_start(out=zcu_h.ap(), in_=zcb[0:32, :]).then_inc(S.d_zcw, 16)
        g.wait_ge(S.pv, 15)
        g.dma_start(out=zcd_h.ap(), in_=zcb[0:86, 0:D]).then_inc(S.d_zcw, 16)
        for m, zh in (("g", zcg_h), ("u", zcu_h)):
            g.wait_ge(S.d_zcw, 16 if m == "g" else 32)
            for bb in range(4):
                g.dma_start(out=bass.AP(zrep[m], bb * 128 * INTER,
                                        [[INTER, 128], [1, INTER]]),
                            in_=bass.AP(zh, 8 * bb * INTER,
                                        [[INTER, 8], [0, 16], [1, INTER]])
                            ).then_inc(S.d_rep, 16)
        if debug:
            slots = [c64, s64, twc, tws, cosB, sinB, omc, oms, gre, gim, cre,
                     cim, None, None, None, None, None, None, None, None,
                     None, None, zre, zim, vrep, None, None, None]
            for i, tl in enumerate(slots):
                if tl is None:
                    continue
                w = tl[:].shape[1]
                g.dma_start(out=bass.AP(dbg_h, i * 64, [[64 * 52, 64], [1, w]]),
                            in_=tl[0:64, 0:w]).then_inc(S.d_zcw, 16)
                dzw_n[0] += 1
            g.dma_start(out=bass.AP(dbg_h, 12 * 64, [[64 * 52, 64], [1, 128]]),
                        in_=aru[:]).then_inc(S.d_zcw, 16)
            dzw_n[0] += 1
            for i in range(8):
                g.dma_start(out=bass.AP(dbg_h, (14 + i) * 64, [[64 * 52, 64], [1, 64]]),
                            in_=kk[:, i * 64:(i + 1) * 64]).then_inc(S.d_zcw, 16)
                dzw_n[0] += 1
            g.dma_start(out=bass.AP(dbg_h, 36 * 64, [[64 * 52, 1], [1, 128]]),
                        in_=brow[0:1, :]).then_inc(S.d_zcw, 16)
            dzw_n[0] += 1
            g.dma_start(out=bass.AP(dbg_h, 40 * 64, [[64 * 52, 1], [1, 512]]),
                        in_=vrow[0:1, 0:512]).then_inc(S.d_zcw, 16)
            dzw_n[0] += 1
        if ST < 2:
            return

        for t in range(NT):
            g.wait_ge(S.c_x1, t + 1)
            W(g, "g", [S.d_x1w0, S.d_x1w1][t % 2], 16 * (t // 2))
            g.dma_start(out=bass.AP(x1_h, t * P * D, [[D, P], [1, D]]),
                        in_=x1b[t % 2][:]).then_inc([S.d_x1w0, S.d_x1w1][t % 2], 16)
        if ST < 3:
            return

        def hts(ncx, tg):
            g.wait_ge(S.c_hT, 4 * ncx + tg + 1)
            W(g, "g", [S.d_hw0, S.d_hw1][tg % 2], 16 * (2 * ncx + tg // 2))
            g.dma_start(out=bass.AP(hT_h, (ncx * P) * TOK + tg * 512,
                                    [[TOK, P], [1, 512]]),
                        in_=hTb[tg % 2][:]).then_inc([S.d_hw0, S.d_hw1][tg % 2], 16)

        def wds(u_i, j):
            kb, o = WD_UNITS[u_i]
            cd = kb * 8 + j
            Kp = DCH[cd][2]
            g.wait_ge(S.c_dqd, u_i * 8 + j + 1)
            W(g, "g", [S.d_wdw0, S.d_wdw1][j % 2], 16 * (u_i * 4 + j // 2))
            g.dma_start(out=bass.AP(wd_h, o * NDCH * P * 512 + cd * P * 512,
                                    [[512, Kp], [1, 512]]),
                        in_=wdo[j % 2][0:Kp, :]).then_inc([S.d_wdw0, S.d_wdw1][j % 2], 16)

        for ncx in range(NCH):
            if ncx > 0:
                for tg in range(4):
                    hts(ncx - 1, tg)
            if ncx < NWU:
                for j in range(8):
                    wds(ncx, j)
        for j in range(8):
            wds(NWU - 2, j)
        for j in range(8):
            wds(NWU - 1, j)
        for tg in range(4):
            hts(NCH - 1, tg)
        if ST < 4:
            return

        for b in range(2):
            for mg in range(8):
                for tt in range(8):
                    idx = (b * 8 + mg) * 8 + tt
                    g.wait_ge(S.c_slabv, idx + 1)
                    W(g, "g", [S.d_slab0, S.d_slab1][idx % 2], 16 * (idx // 2))
                    g.dma_start(out=bass.AP(osc_h, (b * 1024 + tt * P) * D + mg * 512,
                                            [[D, P], [1, 512]]),
                                in_=slab[idx % 2][:]).then_inc([S.d_slab0, S.d_slab1][idx % 2], 16)

        for t in range(NT):
            g.wait_ge(S.c_fo, t + 1)
            W(g, "g", [S.d_out0, S.d_out1][t % 2], 16 * (t // 2))
            g.dma_start(out=bass.AP(out_h, t * P * D, [[D, P], [1, D]]),
                        in_=fout[t % 2][:]).then_inc([S.d_out0, S.d_out1][t % 2], 16)

    @blk.sync
    def _(sy):
        for ap_dst, src in [
            (lam_s[0:1, :], bass.AP(lam_h, 0, [[1, 128]])),
            (pq_s[0:1, 0:128], bass.AP(p_h, 0, [[1, 128]])),
            (pq_s[0:1, 128:256], bass.AP(q_h, 0, [[1, 128]])),
            (vc_s[:], vc_h.ap()),
            (ct_s[0:1, :], bass.AP(ct_h, 0, [[1, 128]])),
            (bp_s[:], bp_h.ap()),
            (dls_s[0:1, 0:1], bass.AP(d_h, 0, [[1, 1]])),
            (dls_s[0:1, 1:2], bass.AP(ls_h, 0, [[1, 1]])),
            (qzg_s[:], zg_h.ap()),
            (qzu_s[:], zu_h.ap()),
            (qzd_s[:], zd_h.ap()),
        ]:
            sy.dma_start(out=ap_dst, in_=src).then_inc(S.d_pro, 16)

        if ST < 2:
            sy.wait_ge(S.d_toep, 16 * 32)
            return

        for t in range(NT):
            W(sy, "sy", S.c_x1, t - 2)
            W(sy, "sy", [S.d_x0, S.d_x1, S.d_x2][t % 3], 16 * (t // 3))
            sy.dma_start(out=xbuf[t % 3][:],
                         in_=bass.AP(x_h, t * P * D, [[D, P], [1, D]])
                         ).then_inc([S.d_x0, S.d_x1, S.d_x2][t % 3], 16)

        if stage == 15:
            sy.wait_ge(S.d_x1w0, 16 * NT // 2)
            sy.wait_ge(S.d_x1w1, 16 * NT // 2)
            return
        W(sy, "sy", S.c_x1, NT)
        sy.wait_ge(S.d_x1w0, 16 * NT // 2)
        sy.wait_ge(S.d_x1w1, 16 * NT // 2)
        for t in range(NT):
            W(sy, "sy", S.c_ln3, t - 1)
            W(sy, "sy", [S.d_x1r0, S.d_x1r1][t % 2], 16 * (t // 2))
            sy.dma_start(out=x1r[t % 2][:],
                         in_=bass.AP(x1_h, t * P * D, [[D, P], [1, D]])
                         ).then_inc([S.d_x1r0, S.d_x1r1][t % 2], 16)
        if ST < 3:
            return

        sy.wait_ge(S.c_ln3, NT)
        sy.wait_ge(S.c_tr1b, 4 * NT)
        sy.wait_ge(S.d_zcw, 16 * dzw_n[0])

        def wd_unit_loads(u_i):
            kb, o = WD_UNITS[u_i]
            Kp = 128 if kb < 10 else 96
            ng = Kp // 16
            W(sy, "sy", S.c_dqd, 8 * (u_i - 1))
            W(sy, "sy", [S.d_qd0, S.d_qd1][u_i % 2], 48 * (u_i // 2))
            dq = [S.d_qd0, S.d_qd1][u_i % 2]
            sy.dma_start(out=qwdb[u_i % 2][0:Kp, :],
                         in_=bass.AP(qwd_h, (128 * kb) * D + o * 512,
                                     [[D, Kp], [1, 512]])).then_inc(dq, 16)
            sy.dma_start(out=adb[u_i % 2][0:Kp, :],
                         in_=bass.AP(sd_h, (8 * kb) * D + o * 512,
                                     [[D, ng], [0, 16], [1, 512]])
                         ).then_inc(dq, 16)
            sy.dma_start(out=zcdb[u_i % 2][0:Kp, :],
                         in_=bass.AP(zcd_h, (8 * kb) * D + o * 512,
                                     [[D, ng], [0, 16], [1, 512]])
                         ).then_inc(dq, 16)

        for ncx in range(NCH):
            W(sy, "sy", S.c_dqu, ncx - 1)
            W(sy, "sy", [S.d_qw0, S.d_qw1][ncx % 2], 32 * (ncx // 2))
            for m, qh in (("g", qwg_h), ("u", qwu_h)):
                sy.dma_start(out=qwb[m][ncx % 2][:],
                             in_=bass.AP(qh, ncx * 128,
                                         [[INTER, 128], [128 * INTER, 4], [1, 128]])
                             ).then_inc([S.d_qw0, S.d_qw1][ncx % 2], 16)
            if ncx == 0:
                sy.wait_ge(S.d_rep, 16 * 16)
            W(sy, "sy", S.c_dqg, ncx - 1)
            W(sy, "sy", S.c_dqu, ncx - 1)
            W(sy, "sy", [S.d_azc0, S.d_azc1][ncx % 2], 64 * (ncx // 2))
            dz = [S.d_azc0, S.d_azc1][ncx % 2]
            for m in ("g", "u"):
                sy.dma_start(out=ab[m][ncx % 2][:],
                             in_=bass.AP(arep[m], ncx * 128,
                                         [[INTER, 128], [128 * INTER, 4], [1, 128]])
                             ).then_inc(dz, 16)
                sy.dma_start(out=zcbuf[m][ncx % 2][:],
                             in_=bass.AP(zrep[m], ncx * 128,
                                         [[INTER, 128], [128 * INTER, 4], [1, 128]])
                             ).then_inc(dz, 16)
            if ncx < NWU:
                wd_unit_loads(ncx)
        wd_unit_loads(NWU - 2)
        wd_unit_loads(NWU - 1)
        if ST < 4:
            sy.wait_ge(S.d_hw0, 16 * 2 * NCH)
            sy.wait_ge(S.d_hw1, 16 * 2 * NCH)
            sy.wait_ge(S.d_wdw0, 16 * 4 * NWU)
            sy.wait_ge(S.d_wdw1, 16 * 4 * NWU)
            return

        sy.wait_ge(S.c_wfu, NCH)
        sy.wait_ge(S.c_dqu, NCH)
        sy.wait_ge(S.c_hT, 4 * NCH)
        sy.wait_ge(S.c_dqd, 8 * NWU)
        sy.wait_ge(S.d_wdw0, 16 * 4 * NWU)
        sy.wait_ge(S.d_wdw1, 16 * 4 * NWU)
        sy.wait_ge(S.d_hw0, 16 * 2 * NCH)
        sy.wait_ge(S.d_hw1, 16 * 2 * NCH)
        for b in range(2):
            for cd, (kb, j, Kp) in enumerate(DCH):
                if b == 1:
                    g4 = cd // 4
                    if g4 <= NGRP - 2:
                        W(sy, "sy", S.c_mm3, 7 * (NGRP - 1) + g4 + 1)
                    else:
                        W(sy, "sy", S.c_ps3, 64)
                sy.dma_start(out=htblk[0:Kp, cd * 1024:(cd + 1) * 1024],
                             in_=bass.AP(hT_h, (1024 * kb + j) * TOK + b * 1024,
                                         [[8 * TOK, Kp], [1, 1024]])
                             ).then_inc(getattr(S, f"d_hr{min(cd // 11, 7)}"), 16)
            for mg in range(8):
                for grp in range(NGRP):
                    gidx = (b * 8 + mg) * NGRP + grp
                    if gidx >= 2:
                        pmgi, pgrp = divmod(gidx - 2, NGRP)
                        if pgrp <= NGRP - 2:
                            W(sy, "sy", S.c_mm3, pmgi * (NGRP - 1) + pgrp + 1)
                        else:
                            W(sy, "sy", S.c_ps3, 8 * (pmgi + 1))
                    W(sy, "sy", [S.d_wdr0, S.d_wdr1][gidx % 2],
                      16 * (gidx // 2))
                    sy.dma_start(out=wdbuf[grp % 2][:],
                                 in_=bass.AP(wd_h,
                                             mg * NDCH * P * 512 + grp * 4 * P * 512,
                                             [[512, 128], [P * 512, 4], [1, 512]])
                                 ).then_inc([S.d_wdr0, S.d_wdr1][gidx % 2], 16)

        sy.wait_ge(S.c_ps3, 128)
        sy.wait_ge(S.d_slab0, 16 * 64)
        sy.wait_ge(S.d_slab1, 16 * 64)
        for t in range(NT):
            W(sy, "sy", S.c_fo, t - 1)
            W(sy, "sy", [S.d_osr0, S.d_osr1][t % 2], 32 * (t // 2))
            sy.dma_start(out=osb[t % 2][:],
                         in_=bass.AP(osc_h, t * P * D, [[D, P], [1, D]])
                         ).then_inc([S.d_osr0, S.d_osr1][t % 2], 16)
            sy.dma_start(out=x1c[t % 2][:],
                         in_=bass.AP(x1_h, t * P * D, [[D, P], [1, D]])
                         ).then_inc([S.d_osr0, S.d_osr1][t % 2], 16)
        sy.wait_ge(S.d_out0, 16 * NT // 2)
        sy.wait_ge(S.d_out1, 16 * NT // 2)

    @blk.scalar
    def _(sc):
        def sr(reads, inst_f, inc=None):
            for tk in reads:
                if tk:
                    W(sc, "sc", tk[0], tk[1])
            inst = inst_f()
            if inc is None:
                tick = bump(S.sself)
                inst.then_inc(S.sself, 1)
            else:
                tick = bump(inc[0], inc[1])
                inst.then_inc(inc[0], inc[1])
            return tick

        sc.wait_ge(S.d_pro, 16 * 11)
        sc.wait_ge(S.pg, 1)
        sr([], lambda: sc.activation(out=step_t[:], in_=dls_s[0:1, 1:2],
                                     func=AF.Exp), inc=(S.ps, 1))
        sc.wait_ge(S.d_kst, 16 * 128)
        for d_ in range(32):
            sc.dma_start(out=toep[:, (3 + d_) * 128:(4 + d_) * 128],
                         in_=bass.AP(kst_h, 128 + 128 * d_, [[4352, 128], [1, 128]])
                         ).then_inc(S.d_toep, 16)

        def ln_scalar(gi, stat, src, dst, c_in, c_in_v, waits, inc):
            st = stat[gi % 2]
            sc.wait_ge(c_in, c_in_v)
            for sem, v_ in waits:
                W(sc, "sc", sem, v_)
            i1 = sr([], lambda: sc.activation(out=dst[:], in_=src[:],
                                              func=AF.Square, bias=st[:, 1:2],
                                              scale=1.0, accum_out=st[:, 2:3]))
            i2 = sr([i1], lambda: sc.activation(out=st[:, 3:4], in_=st[:, 2:3],
                                                func=AF.Ln, scale=1.0 / D,
                                                bias=EPS))
            i3 = sr([i2], lambda: sc.activation(out=st[:, 4:5], in_=st[:, 3:4],
                                                func=AF.Exp, scale=-0.5))
            i4 = sr([i3], lambda: sc.mul(st[:, 5:6], st[:, 1:2], st[:, 4:5]))
            sr([i4], lambda: sc.activation(out=dst[:], in_=src[:],
                                           func=AF.Identity,
                                           scale=st[:, 4:5], bias=st[:, 5:6]),
               inc=inc)

        if ST < 2:
            return
        if debug:
            sc.wait_ge(S.d_zcw, 16 * dzw_n[0])
        for t in range(NT):
            ln_scalar(t, stats["s1"], xbuf[t % 3], u_f[t % 2],
                      S.c_v1, t + 1, [(S.c_yassb, 8 * (t - 2) + 8)],
                      (S.c_u, 1))
            if t >= 1:
                ln_scalar(t - 1, stats["s2"], y_sb[(t - 1) % 2], ln2n,
                          S.c_v2, t, [(S.c_x1, t - 1)], (S.c_ln2, 1))
        ln_scalar(NT - 1, stats["s2"], y_sb[(NT - 1) % 2], ln2n,
                  S.c_v2, NT, [(S.c_x1, NT - 1)], (S.c_ln2, 1))

        if stage == 15:
            return
        for t in range(NT):
            ln_scalar(t, stats["s3"], x1r[t % 2], x1nb[t % 2],
                      S.c_v3, t + 1, [(S.c_tr1b, 4 * (t - 2) + 4)],
                      (S.c_ln3, 1))
        if ST < 3:
            return

        for ncx in range(NCH):
            for tg in range(4):
                if tg < 3:
                    sc.wait_ge(S.c_psg, 3 * ncx + tg + 1)
                else:
                    sc.wait_ge(S.c_wfg, ncx + 1)
                W(sc, "sc", S.c_hT, 4 * ncx + tg - 1)
                sr([], lambda tg=tg: sc.activation(out=silu_sb[tg % 2][:],
                                                   in_=pbank(tg)[:, :],
                                                   func=AF.Silu),
                   inc=(S.c_silu, 1))
        if ST < 4:
            return

        for t in range(NT):
            ln_scalar(t, stats["s4"], osb[t % 2], ln4n,
                      S.c_v4, t + 1, [(S.c_fo, t)], (S.c_ln4, 1))

    @blk.vector
    def _(v):
        def vr(reads, inst_f, inc=None):
            for tk in reads:
                if tk:
                    W(v, "v", tk[0], tk[1])
            inst = inst_f()
            if inc is None:
                tick = bump(S.vself)
                inst.then_inc(S.vself, 1)
            else:
                tick = bump(inc[0], inc[1])
                inst.then_inc(inc[0], inc[1])
            return tick

        v.wait_ge(S.pg, 1)
        v.wait_ge(S.d_pro, 16 * 11)

        def trig2(dst_c, dst_s, r_tick, r_src):
            PI_HI = 3.14159274101257324
            PI_ERR = float(np.float64(np.float32(np.pi)) - np.float64(np.pi))
            tvf = vr([r_tick], lambda: v.tensor_scalar(
                out=pB[:], in0=r_src[:], scalar1=6.28318530717958647, scalar2=None,
                op0=ALU.mult))
            tn1 = vr([r_tick], lambda: v.tensor_scalar(
                out=pD[:], in0=r_src[:], scalar1=0.25, scalar2=None, op0=ALU.is_gt))
            tn2 = vr([r_tick], lambda: v.tensor_scalar(
                out=pE[:], in0=r_src[:], scalar1=0.75, scalar2=None, op0=ALU.is_gt))
            tnn = vr([tn1, tn2], lambda: v.tensor_tensor(
                out=pF[:], in0=pD[:], in1=pE[:], op=ALU.add))
            tv1 = vr([tnn, tvf], lambda: v.scalar_tensor_tensor(
                out=pA[:], in0=pF[:], scalar=-PI_HI, in1=pB[:],
                op0=ALU.mult, op1=ALU.add))
            tvv = vr([tv1], lambda: v.scalar_tensor_tensor(
                out=pB[:], in0=pF[:], scalar=PI_ERR, in1=pA[:],
                op0=ALU.mult, op1=ALU.add))
            tsg = vr([tn1, tn2], lambda: v.tensor_tensor(
                out=pC[:], in0=pD[:], in1=pE[:], op=ALU.subtract))
            tsg = vr([tsg], lambda: v.tensor_scalar(
                out=pE[:], in0=pC[:], scalar1=-2.0, scalar2=1.0,
                op0=ALU.mult, op1=ALU.add))
            tv2 = vr([tvv], lambda: v.tensor_tensor(
                out=pA[:], in0=pB[:], in1=pB[:], op=ALU.mult))

            def horner(coefs):
                tk = vr([tv2], lambda: v.tensor_scalar(
                    out=pC[:], in0=pA[:], scalar1=float(coefs[-1]),
                    scalar2=float(coefs[-2]), op0=ALU.mult, op1=ALU.add))
                for c in coefs[-3::-1]:
                    tk2 = vr([tk], lambda: v.tensor_tensor(
                        out=pD[:], in0=pC[:], in1=pA[:], op=ALU.mult))
                    tk = vr([tk2], lambda c=c: v.tensor_scalar(
                        out=pC[:], in0=pD[:], scalar1=float(c), scalar2=None,
                        op0=ALU.add))
                return tk

            tks = horner(SINC)
            tsp = vr([tks, tvv], lambda: v.tensor_tensor(
                out=pD[:], in0=pC[:], in1=pB[:], op=ALU.mult))
            ts_ = vr([tsp, tsg], lambda: v.tensor_tensor(
                out=dst_s[:], in0=pD[:], in1=pE[:], op=ALU.mult))
            tkc = horner(COSC)
            tc_ = vr([tkc], lambda: v.tensor_tensor(
                out=dst_c[:], in0=pC[:], in1=pE[:], op=ALU.mult))
            return tc_, ts_

        gi32 = grid[0:64, 0:64].bitcast(I32)
        gi32b = grid[0:64, 64:128].bitcast(I32)
        tpf = vr([], lambda: v.tensor_scalar(
            out=gi32, in0=colidx[:], scalar1=rowf[:, :], scalar2=None, op0=ALU.mult))
        tpm = vr([tpf], lambda: v.tensor_scalar(
            out=gi32b, in0=gi32, scalar1=63, scalar2=None, op0=ALU.bitwise_and))
        trf = vr([tpm], lambda: v.tensor_copy(out=pF[:], in_=gi32b))
        trf = vr([trf], lambda: v.tensor_scalar(
            out=colf[:], in0=pF[:], scalar1=1.0 / 64, scalar2=None, op0=ALU.mult))
        tc64, ts64 = trig2(c64, s64, trf, colf)
        trf2 = vr([tpf, tc64, ts64], lambda: v.tensor_copy(out=pF[:], in_=gi32))
        trf2 = vr([trf2], lambda: v.tensor_scalar(
            out=colf[:], in0=pF[:], scalar1=1.0 / 4096, scalar2=None, op0=ALU.mult))
        ttwc, ttws = trig2(twc, tws, trf2, colf)
        tcf = vr([ttwc, ttws], lambda: v.tensor_copy(out=pF[:], in_=colidx[:]))
        tcf = vr([tcf], lambda: v.tensor_scalar(
            out=colf[:], in0=pF[:], scalar1=1.0 / 4096, scalar2=None, op0=ALU.mult))
        tcB, tsB = trig2(cosB, sinB, tcf, colf)
        vr([tcB, tsB], lambda: v.memset(toep[:, 0:384], 0.0), inc=(S.pv, 1))

        v.wait_ge(S.ps, 1)
        tiv = vr([], lambda: v.reciprocal(out=pF[0:1, 0:1], in_=step_t[:]))
        vr([tiv], lambda: v.tensor_scalar(
            out=inv2_t[:], in0=pF[0:1, 0:1], scalar1=2.0, scalar2=None,
            op0=ALU.mult), inc=(S.pv, 1))

        v.wait_ge(S.pt, 1)
        te1 = vr([], lambda: v.tensor_copy(out=g2s[:], in_=pbg[0][0:64, 0:1]))
        vr([te1], lambda: v.tensor_copy(out=dcol[:], in_=pbg[1][0:128, 0:1]),
           inc=(S.pv, 1))
        v.wait_ge(S.pt, 2)
        tl1 = vr([], lambda: v.tensor_copy(out=lrep[:, 0:64], in_=pbg[2][0:64, 0:64]))
        tl2 = vr([tl1], lambda: v.tensor_copy(out=lrep[:, 64:128],
                                              in_=pbg[3][0:64, 0:64]),
                 inc=(S.pv, 1))
        v.wait_ge(S.pt, 3)
        tv1 = vr([], lambda: v.tensor_copy(out=vct[:, 0:64], in_=pbg[4][0:64, 0:64]))
        vr([tv1], lambda: v.tensor_copy(out=vct[:, 64:128], in_=pbg[5][0:64, 0:64]),
           inc=(S.pv, 1))
        v.wait_ge(S.pt, 4)
        tb0 = vr([], lambda: v.tensor_copy(out=bcol[:, 0:1], in_=pbg[2][0:64, 0:1]))
        vr([tb0], lambda: v.tensor_copy(out=bcol[:, 1:2], in_=pbg[3][0:64, 0:1]),
           inc=(S.pv, 1))

        cA = c64[:, 1:2]
        sA = s64[:, 1:2]
        t1 = vr([tcB], lambda: v.tensor_scalar(
            out=pA[:], in0=cosB[:], scalar1=cA, scalar2=None, op0=ALU.mult))
        t2 = vr([tsB], lambda: v.tensor_scalar(
            out=pB[:], in0=sinB[:], scalar1=sA, scalar2=None, op0=ALU.mult))
        tom = vr([t1, t2], lambda: v.tensor_tensor(
            out=omc[:], in0=pA[:], in1=pB[:], op=ALU.subtract))
        t3 = vr([], lambda: v.tensor_scalar(
            out=pC[:], in0=cosB[:], scalar1=sA, scalar2=None, op0=ALU.mult))
        t4 = vr([], lambda: v.tensor_scalar(
            out=pD[:], in0=sinB[:], scalar1=cA, scalar2=None, op0=ALU.mult))
        t5 = vr([t3, t4], lambda: v.tensor_tensor(
            out=pE[:], in0=pC[:], in1=pD[:], op=ALU.add))
        tos = vr([t5], lambda: v.tensor_scalar(
            out=oms[:], in0=pE[:], scalar1=-1.0, scalar2=None, op0=ALU.mult))

        tdre = vr([tom], lambda: v.tensor_scalar(
            out=pA[:], in0=omc[:], scalar1=1.0, scalar2=None, op0=ALU.add))
        tq1 = vr([tdre], lambda: v.tensor_tensor(
            out=pB[:], in0=pA[:], in1=pA[:], op=ALU.mult))
        tq2 = vr([tos], lambda: v.tensor_tensor(
            out=pC[:], in0=oms[:], in1=oms[:], op=ALU.mult))
        tq3 = vr([tq1, tq2], lambda: v.tensor_tensor(
            out=pF[:], in0=pB[:], in1=pC[:], op=ALU.add))
        trc = vr([tq3], lambda: v.reciprocal(out=pB[:], in_=pF[:]))
        trdre = vr([trc], lambda: v.tensor_tensor(
            out=pC[:], in0=pA[:], in1=pB[:], op=ALU.mult))
        trdim = vr([trc], lambda: v.tensor_tensor(
            out=pD[:], in0=oms[:], in1=pB[:], op=ALU.mult))
        tcre = vr([trdre], lambda: v.tensor_scalar(
            out=cre[:], in0=pC[:], scalar1=2.0 / 4096, scalar2=None, op0=ALU.mult))
        tcim = vr([trdim], lambda: v.tensor_scalar(
            out=cim[:], in0=pD[:], scalar1=-2.0 / 4096, scalar2=None, op0=ALU.mult))
        tnre = vr([tom], lambda: v.tensor_scalar(
            out=pE[:], in0=omc[:], scalar1=-1.0, scalar2=1.0,
            op0=ALU.mult, op1=ALU.add))
        ta1 = vr([tnre, trdre], lambda: v.tensor_tensor(
            out=pA[:], in0=pE[:], in1=pC[:], op=ALU.mult))
        ta2 = vr([trdim], lambda: v.tensor_tensor(
            out=pB[:], in0=oms[:], in1=pD[:], op=ALU.mult))
        tqre = vr([ta1, ta2], lambda: v.tensor_tensor(
            out=pF[:], in0=pA[:], in1=pB[:], op=ALU.subtract))
        tgre = vr([tqre], lambda: v.tensor_scalar(
            out=gre[:], in0=pF[:], scalar1=g2s[:, :], scalar2=None, op0=ALU.mult))
        tb1 = vr([tnre, trdim], lambda: v.tensor_tensor(
            out=pA[:], in0=pE[:], in1=pD[:], op=ALU.mult))
        tb2 = vr([trdre], lambda: v.tensor_tensor(
            out=pB[:], in0=oms[:], in1=pC[:], op=ALU.mult))
        tqim = vr([tb1, tb2], lambda: v.tensor_tensor(
            out=pF[:], in0=pA[:], in1=pB[:], op=ALU.add))
        tqim = vr([tqim], lambda: v.tensor_scalar(
            out=pE[:], in0=pF[:], scalar1=-1.0, scalar2=None, op0=ALU.mult))
        tgim = vr([tqim], lambda: v.tensor_scalar(
            out=gim[:], in0=pE[:], scalar1=g2s[:, :], scalar2=None, op0=ALU.mult))

        v.wait_ge(S.pt, 5)
        tb1_ = vr([], lambda: v.tensor_copy(out=brow[0:1, 0:64],
                                            in_=pbg[2][0:1, 0:64]))
        tb2_ = vr([tb1_], lambda: v.tensor_copy(out=brow[0:1, 64:128],
                                                in_=pbg[3][0:1, 0:64]))
        ctRe, ctIm = ct_s[0:1, 0:128:2], ct_s[0:1, 1:128:2]
        pRe, pIm = pq_s[0:1, 0:128:2], pq_s[0:1, 1:128:2]
        qRe, qIm = pq_s[0:1, 128:256:2], pq_s[0:1, 129:256:2]
        bRe, bIm = brow[0:1, 0:64], brow[0:1, 64:128]
        row = lambda i: vrow[0:1, i * 64:(i + 1) * 64]
        tmpr = grid[0:1, 0:64]
        tmpi = grid[0:1, 64:128]

        def cmulconj(ar, ai, br, bi, idx, deps):
            ta_ = vr(deps, lambda: v.tensor_tensor(out=tmpr, in0=ar, in1=br,
                                                   op=ALU.mult))
            tb_ = vr(deps, lambda: v.tensor_tensor(out=tmpi, in0=ai, in1=bi,
                                                   op=ALU.mult))
            tre = vr([ta_, tb_], lambda: v.tensor_tensor(
                out=row(2 * idx), in0=tmpr, in1=tmpi, op=ALU.add))
            tc_ = vr([tre], lambda: v.tensor_tensor(out=tmpr, in0=ar, in1=bi,
                                                    op=ALU.mult))
            td_ = vr([tre], lambda: v.tensor_tensor(out=tmpi, in0=ai, in1=br,
                                                    op=ALU.mult))
            return vr([tc_, td_], lambda: v.tensor_tensor(
                out=row(2 * idx + 1), in0=tmpr, in1=tmpi, op=ALU.subtract))

        tva = cmulconj(ctRe, ctIm, bRe, bIm, 0, [tb2_])
        tvb = cmulconj(ctRe, ctIm, pRe, pIm, 1, [tva])
        tvc = cmulconj(qRe, qIm, bRe, bIm, 2, [tvb])
        tvd = cmulconj(qRe, qIm, pRe, pIm, 3, [tvc])
        vr([tvd], lambda: v.tensor_copy(out=pF[0:1, 0:1], in_=vrow[0:1, 0:1]),
           inc=(S.pv, 1))

        v.wait_ge(S.pt, 6)
        tvr_ = vr([], lambda: v.tensor_copy(out=vrep[:], in_=pbg[6][0:64, 0:512]),
                  inc=(S.pv, 1))

        pstep = lrep[:].ap[0][0]
        lre_b = bass.AP(lrep[:].tensor, lrep[:].offset, [[pstep, 64], [0, 64], [1, 64]])
        lim_b = bass.AP(lrep[:].tensor, lrep[:].offset + 64,
                        [[pstep, 64], [0, 64], [1, 64]])
        def g3(t_):
            a = t_[:]
            return a.rearrange("p (a b) -> p a b", a=64)
        gre_b = bass.AP(gre[:].tensor, gre[:].offset,
                        [[gre[:].ap[0][0], 64], [1, 64], [0, 64]])
        gim_b = bass.AP(gim[:].tensor, gim[:].offset,
                        [[gim[:].ap[0][0], 64], [1, 64], [0, 64]])
        tdr = vr([tgre, tl2], lambda: v.tensor_tensor(
            out=g3(dre3), in0=gre_b, in1=lre_b, op=ALU.subtract))
        tdi = vr([tgim], lambda: v.tensor_tensor(
            out=g3(dim3), in0=gim_b, in1=lim_b, op=ALU.subtract))
        tg1 = vr([tdr], lambda: v.tensor_tensor(
            out=grid[0:64, :], in0=dre3[:], in1=dre3[:], op=ALU.mult))
        tg2 = vr([tdi], lambda: v.tensor_tensor(
            out=rre3[:], in0=dim3[:], in1=dim3[:], op=ALU.mult))
        tg3_ = vr([tg1, tg2], lambda: v.tensor_tensor(
            out=rim3[:], in0=grid[0:64, :], in1=rre3[:], op=ALU.add))
        tg4 = vr([tg3_], lambda: v.reciprocal(out=grid[0:64, :], in_=rim3[:]))
        tg5 = vr([tg4], lambda: v.tensor_tensor(
            out=rre3[:], in0=dre3[:], in1=grid[0:64, :], op=ALU.mult))
        tg6 = vr([tg4], lambda: v.tensor_tensor(
            out=rim3[:], in0=dim3[:], in1=grid[0:64, :], op=ALU.mult))

        vrep_b = lambda i: bass.AP(vrep[:].tensor, vrep[:].offset + 64 * i,
                                   [[vrep[:].ap[0][0], 64], [0, 64], [1, 64]])
        kkv = lambda i: kk[:, i * 64:(i + 1) * 64]
        tprev = [tg5, tg6, tvr_]
        for kv in range(4):
            t1_ = vr(tprev, lambda kv=kv: v.tensor_tensor(
                out=g3(dre3), in0=g3(rre3), in1=vrep_b(2 * kv), op=ALU.mult))
            t2_ = vr(tprev, lambda kv=kv: v.tensor_tensor(
                out=g3(dim3), in0=g3(rim3), in1=vrep_b(2 * kv + 1), op=ALU.mult))
            t3a = vr([t1_, t2_], lambda: v.tensor_tensor(
                out=grid[0:64, :], in0=dre3[:], in1=dim3[:], op=ALU.add))
            tre = vr([t3a], lambda kv=kv: v.tensor_reduce(
                out=kkv(2 * kv).unsqueeze(2),
                in_=grid[0:64, :].rearrange("p (a b) -> p a b", a=64),
                axis=AX.X, op=ALU.add))
            t4_ = vr([t3a], lambda kv=kv: v.tensor_tensor(
                out=g3(dre3), in0=g3(rre3), in1=vrep_b(2 * kv + 1), op=ALU.mult))
            t5_ = vr([t3a], lambda kv=kv: v.tensor_tensor(
                out=g3(dim3), in0=g3(rim3), in1=vrep_b(2 * kv), op=ALU.mult))
            t6_ = vr([t4_, t5_, tre], lambda: v.tensor_tensor(
                out=grid[0:64, :], in0=dre3[:], in1=dim3[:], op=ALU.subtract))
            tim = vr([t6_], lambda kv=kv: v.tensor_reduce(
                out=kkv(2 * kv + 1).unsqueeze(2),
                in_=grid[0:64, :].rearrange("p (a b) -> p a b", a=64),
                axis=AX.X, op=ALU.add))
            tprev = [tre, tim]

        k00r, k00i = kkv(0), kkv(1)
        k01r, k01i = kkv(2), kkv(3)
        k10r, k10i = kkv(4), kkv(5)
        k11r, k11i = kkv(6), kkv(7)
        tw1 = vr(tprev, lambda: v.tensor_scalar(
            out=pA[:], in0=k11r, scalar1=1.0, scalar2=None, op0=ALU.add))
        tq1_ = vr([tw1], lambda: v.tensor_tensor(out=pB[:], in0=pA[:], in1=pA[:],
                                                 op=ALU.mult))
        tq2_ = vr(tprev, lambda: v.tensor_tensor(out=pC[:], in0=k11i, in1=k11i,
                                                 op=ALU.mult))
        tq3_ = vr([tq1_, tq2_], lambda: v.tensor_tensor(out=pD[:], in0=pB[:],
                                                        in1=pC[:], op=ALU.add))
        tqr = vr([tq3_], lambda: v.reciprocal(out=pB[:], in_=pD[:]))
        tiwr = vr([tqr], lambda: v.tensor_tensor(out=pC[:], in0=pA[:], in1=pB[:],
                                                 op=ALU.mult))
        tiwi = vr([tqr], lambda: v.tensor_tensor(out=pD[:], in0=k11i, in1=pB[:],
                                                 op=ALU.mult))
        tt1 = vr([tiwr], lambda: v.tensor_tensor(out=pE[:], in0=k01r, in1=k10r,
                                                 op=ALU.mult))
        tt2 = vr([tiwi], lambda: v.tensor_tensor(out=pF[:], in0=k01i, in1=k10i,
                                                 op=ALU.mult))
        ttr = vr([tt1, tt2], lambda: v.tensor_tensor(out=pA[:], in0=pE[:],
                                                     in1=pF[:], op=ALU.subtract))
        tt3 = vr([ttr], lambda: v.tensor_tensor(out=pE[:], in0=k01r, in1=k10i,
                                                op=ALU.mult))
        tt4 = vr([ttr], lambda: v.tensor_tensor(out=pF[:], in0=k01i, in1=k10r,
                                                op=ALU.mult))
        tti = vr([tt3, tt4], lambda: v.tensor_tensor(out=pB[:], in0=pE[:],
                                                     in1=pF[:], op=ALU.add))
        ts1_ = vr([tti], lambda: v.tensor_tensor(out=pE[:], in0=pA[:], in1=pC[:],
                                                 op=ALU.mult))
        ts2_ = vr([tti], lambda: v.tensor_tensor(out=pF[:], in0=pB[:], in1=pD[:],
                                                 op=ALU.mult))
        tsre = vr([ts1_, ts2_], lambda: v.tensor_tensor(
            out=grid[0:64, 0:64], in0=pE[:], in1=pF[:], op=ALU.add))
        ts3_ = vr([tsre], lambda: v.tensor_tensor(out=pE[:], in0=pB[:], in1=pC[:],
                                                  op=ALU.mult))
        ts4_ = vr([tsre], lambda: v.tensor_tensor(out=pF[:], in0=pA[:], in1=pD[:],
                                                  op=ALU.mult))
        tsim = vr([ts3_, ts4_], lambda: v.tensor_tensor(
            out=grid[0:64, 64:128], in0=pE[:], in1=pF[:], op=ALU.subtract))
        trr = vr([tsre], lambda: v.tensor_tensor(
            out=pA[:], in0=k00r, in1=grid[0:64, 0:64], op=ALU.subtract))
        tri = vr([tsim], lambda: v.tensor_tensor(
            out=pB[:], in0=k00i, in1=grid[0:64, 64:128], op=ALU.subtract))
        ta1_ = vr([trr, tcre], lambda: v.tensor_tensor(out=pC[:], in0=cre[:],
                                                       in1=pA[:], op=ALU.mult))
        ta2_ = vr([tri, tcim], lambda: v.tensor_tensor(out=pD[:], in0=cim[:],
                                                       in1=pB[:], op=ALU.mult))
        tare = vr([ta1_, ta2_], lambda: v.tensor_tensor(
            out=aru[:, 0:64], in0=pC[:], in1=pD[:], op=ALU.subtract))
        ta3_ = vr([tare], lambda: v.tensor_tensor(out=pC[:], in0=cre[:], in1=pB[:],
                                                  op=ALU.mult))
        ta4_ = vr([tare], lambda: v.tensor_tensor(out=pD[:], in0=cim[:], in1=pA[:],
                                                  op=ALU.mult))
        vr([ta3_, ta4_], lambda: v.tensor_tensor(
            out=aru[:, 64:128], in0=pC[:], in1=pD[:], op=ALU.add),
           inc=(S.pv, 1))

        v.wait_ge(S.pt, 7)
        tm1 = vr([], lambda: v.tensor_copy(out=m1s[:], in_=pbg[0][0:64, 0:128]))
        tm2 = vr([tm1], lambda: v.tensor_copy(out=m2s[:], in_=pbg[1][0:64, 0:128]))
        ty1 = vr([tm2], lambda: v.tensor_tensor(
            out=yre[:], in0=m1s[:, 0:64], in1=m2s[:, 64:128], op=ALU.subtract))
        ty2 = vr([tm2], lambda: v.tensor_tensor(
            out=yim[:], in0=m1s[:, 64:128], in1=m2s[:, 0:64], op=ALU.add))
        tz1 = vr([ty1], lambda: v.tensor_tensor(out=pA[:], in0=yre[:], in1=twc[:],
                                                op=ALU.mult))
        tz2 = vr([ty2], lambda: v.tensor_tensor(out=pB[:], in0=yim[:], in1=tws[:],
                                                op=ALU.mult))
        tzr = vr([tz1, tz2], lambda: v.tensor_tensor(out=zre[:], in0=pA[:],
                                                     in1=pB[:], op=ALU.subtract))
        tz3 = vr([ty1], lambda: v.tensor_tensor(out=pC[:], in0=yre[:], in1=tws[:],
                                                op=ALU.mult))
        tz4 = vr([ty2], lambda: v.tensor_tensor(out=pD[:], in0=yim[:], in1=twc[:],
                                                op=ALU.mult))
        tzi = vr([tz3, tz4], lambda: v.tensor_tensor(out=zim[:], in0=pC[:],
                                                     in1=pD[:], op=ALU.add))
        vr([tzr, tzi], lambda: v.tensor_copy(out=pF[0:1, 0:1], in_=zre[0:1, 0:1]),
           inc=(S.pv, 1))
        v.wait_ge(S.pt, 8)
        tzt1 = vr([], lambda: v.tensor_copy(out=zret[:], in_=pbg[2][0:64, 0:64]))
        vr([tzt1], lambda: v.tensor_scalar(
            out=zimtn[:], in0=pbg[3][0:64, 0:64], scalar1=-1.0, scalar2=None,
            op0=ALU.mult), inc=(S.pv, 1))
        v.wait_ge(S.pt, 9)
        vr([], lambda: v.tensor_copy(out=ktb[:], in_=pbg[4][0:64, 0:64]),
           inc=(S.pv, 1))

        def unpack_z(src, rows, width, dzw_wait, prev=None):
            zstep = zci[:].ap[0][0]
            tks = []
            for jn in range(8):
                tks.append(vr([prev], lambda jn=jn: v.tensor_scalar(
                    out=bass.AP(zci[:].tensor, zci[:].offset + jn,
                                [[zstep, rows], [8, width]]),
                    in0=src[0:rows, 0:width], scalar1=4 * jn, scalar2=15,
                    op0=ALU.logical_shift_right, op1=ALU.bitwise_and)))
            W(v, "v", S.d_zcw, dzw_wait)
            return vr(tks, lambda: v.tensor_scalar(
                out=zcb[0:rows, 0:width * 8], in0=zci[0:rows, 0:width * 8],
                scalar1=1, scalar2=None, op0=ALU.add), inc=(S.pv, 1))

        tz1 = unpack_z(qzg_s, 32, 1376, 0)
        tz2 = unpack_z(qzu_s, 32, 1376, 16, prev=tz1)
        unpack_z(qzd_s, 86, 512, 32, prev=tz2)
        if ST < 2:
            return

        v.wait_ge(S.d_zcw, 16 * dzw_n[0])

        def yass_half(tm, h):
            la = 0
            v.wait_ge(S.c_conv, 2 * tm + h + 1)
            for gg in range(4):
                g_ = 4 * h + gg
                la = vr([], lambda tm=tm, g_=g_, gg=gg: v.scalar_tensor_tensor(
                    out=y_sb[tm % 2][:, g_ * 512:(g_ + 1) * 512],
                    in0=u_f[tm % 2][:, g_ * 512:(g_ + 1) * 512],
                    scalar=dcol[:, :], in1=pbank(4 + gg)[:, :],
                    op0=ALU.mult, op1=ALU.add),
                    inc=(S.c_yassb, 1))
            return la

        for t in range(NT + 2):
            if 0 <= t - 1 < NT:
                yass_half(t - 1, 0)
            if t < NT:
                st1 = stats["s1"][t % 2]
                v.wait_ge([S.d_x0, S.d_x1, S.d_x2][t % 3], 16 * (t // 3 + 1))
                W(v, "v", S.c_u, t - 1)
                t_sum = vr([], lambda t=t: v.tensor_reduce(
                    out=st1[:, 0:1], in_=xbuf[t % 3][:], axis=AX.X, op=ALU.add))
                vr([t_sum], lambda t=t: v.tensor_scalar(
                    out=st1[:, 1:2], in0=st1[:, 0:1], scalar1=-1.0 / D,
                    scalar2=None, op0=ALU.mult), inc=(S.c_v1, 1))
            if 0 <= t - 2 < NT:
                tm = t - 2
                v.wait_ge(S.c_ln2, tm + 1)
                W(v, "v", [S.d_x1w0, S.d_x1w1][tm % 2], 16 * (tm // 2))
                vr([], lambda tm=tm: v.tensor_tensor(
                    out=x1b[tm % 2][:], in0=xbuf[tm % 3][:], in1=ln2n[:],
                    op=ALU.add), inc=(S.c_x1, 1))
            if t < NT:
                v.wait_ge(S.c_u, t + 1)
                W(v, "v", S.c_tr1a, 4 * (t - 2) + 4)
                vr([], lambda t=t: v.tensor_copy(out=ubf[t % 2][:],
                                                 in_=u_f[t % 2][:]),
                   inc=(S.c_ub, 1))
            if 0 <= t - 1 < NT:
                tm = t - 1
                st2 = stats["s2"][tm % 2]
                last_ass = yass_half(tm, 1)
                W(v, "v", S.c_ln2, tm - 1)
                t_s2 = vr([last_ass], lambda tm=tm: v.tensor_reduce(
                    out=st2[:, 0:1], in_=y_sb[tm % 2][:], axis=AX.X, op=ALU.add))
                vr([t_s2], lambda tm=tm: v.tensor_scalar(
                    out=st2[:, 1:2], in0=st2[:, 0:1], scalar1=-1.0 / D,
                    scalar2=None, op0=ALU.mult), inc=(S.c_v2, 1))
            if t < NT:
                W(v, "v", S.c_conv, 2 * (t - 2) + 2)
                for grp in range(4):
                    gi = 4 * t + grp
                    v.wait_ge(S.c_tr1a, gi + 1)
                    vr([], lambda t=t, grp=grp, gi=gi: v.tensor_copy(
                        out=uT[t % 2][:, grp * 1024:(grp + 1) * 1024],
                        in_=pbank_bf(gi % 2)[:, 0:1024]),
                       inc=(S.c_uTe, 1))

        if stage == 15:
            return
        v.wait_ge(S.d_x1w0, 16 * NT // 2)
        v.wait_ge(S.d_x1w1, 16 * NT // 2)
        xstep = xt[:].ap[0][0]
        for t in range(NT):
            st3 = stats["s3"][t % 2]
            v.wait_ge([S.d_x1r0, S.d_x1r1][t % 2], 16 * (t // 2 + 1))
            W(v, "v", S.c_ln3, t - 1)
            t_s3 = vr([], lambda t=t: v.tensor_reduce(
                out=st3[:, 0:1], in_=x1r[t % 2][:], axis=AX.X, op=ALU.add))
            vr([t_s3], lambda t=t: v.tensor_scalar(
                out=st3[:, 1:2], in0=st3[:, 0:1], scalar1=-1.0 / D, scalar2=None,
                op0=ALU.mult), inc=(S.c_v3, 1))
            for grp in range(4):
                gi = 4 * t + grp
                v.wait_ge(S.c_tr1b, gi + 1)
                vr([], lambda t=t, grp=grp, gi=gi: v.tensor_copy(
                    out=bass.AP(xt[:].tensor,
                                xt[:].offset + (8 * grp) * TOK + t * P,
                                [[xstep, P], [TOK, 8], [1, P]]),
                    in_=pbank_bf(gi % 2)[:, 0:1024]),
                   inc=(S.c_xTe, 1))
        if ST < 3:
            return

        def deq_gu(m, ncx, c_wf, c_dq, qfree):
            W(v, "v", c_wf, ncx)
            v.wait_ge([S.d_qw0, S.d_qw1][ncx % 2], 16 * 2 * (ncx // 2 + 1))
            v.wait_ge([S.d_azc0, S.d_azc1][ncx % 2], 64 * (ncx // 2 + 1))
            last = {}
            for j in range(8):
                tsh = vr([], lambda m=m, ncx=ncx, j=j: v.tensor_scalar(
                    out=w4i[j % 2][:], in0=qwb[m][ncx % 2][:], scalar1=4 * j,
                    scalar2=15, op0=ALU.logical_shift_right, op1=ALU.bitwise_and))
                tsub = vr([tsh, last.get(("f", j % 2), 0)],
                          lambda m=m, ncx=ncx, j=j: v.scalar_tensor_tensor(
                    out=wfb[j % 2][:], in0=w4i[j % 2][:], scalar=1.0,
                    in1=zcbuf[m][ncx % 2][:], op0=ALU.mult, op1=ALU.subtract))
                vr([tsub], lambda m=m, ncx=ncx, j=j: v.tensor_tensor(
                    out=wtil[m][:, j * 512:(j + 1) * 512], in0=wfb[j % 2][:],
                    in1=ab[m][ncx % 2][:], op=ALU.mult),
                   inc=((c_dq, 1) if j == 7 else None))
                last[("f", j % 2)] = tsub

        def deq_wd(u_i):
            kb, o = WD_UNITS[u_i]
            Kp = 128 if kb < 10 else 96
            v.wait_ge([S.d_qd0, S.d_qd1][u_i % 2], 16 * 3 * (u_i // 2 + 1))
            tprevf = 0
            for j in range(8):
                W(v, "v", [S.d_wdw0, S.d_wdw1][j % 2],
                  16 * (u_i * 4 + j // 2))
                tsh = vr([tprevf], lambda u_i=u_i, j=j, Kp=Kp: v.tensor_scalar(
                    out=w4d[0:Kp, :], in0=qwdb[u_i % 2][0:Kp, :], scalar1=4 * j,
                    scalar2=15, op0=ALU.logical_shift_right, op1=ALU.bitwise_and))
                tsub = vr([tsh], lambda u_i=u_i, Kp=Kp: v.scalar_tensor_tensor(
                    out=wfd[0:Kp, :], in0=w4d[0:Kp, :], scalar=1.0,
                    in1=zcdb[u_i % 2][0:Kp, :], op0=ALU.mult, op1=ALU.subtract))
                vr([tsub], lambda u_i=u_i, j=j, Kp=Kp: v.tensor_tensor(
                    out=wdo[j % 2][0:Kp, :], in0=wfd[0:Kp, :],
                    in1=adb[u_i % 2][0:Kp, :], op=ALU.mult),
                   inc=(S.c_dqd, 1))
                tprevf = tsub

        def ht_evac(ncx):
            for tg in range(4):
                if tg < 3:
                    v.wait_ge(S.c_psu, 3 * ncx + tg + 1)
                else:
                    v.wait_ge(S.c_wfu, ncx + 1)
                v.wait_ge(S.c_silu, 4 * ncx + tg + 1)
                W(v, "v", [S.d_hw0, S.d_hw1][tg % 2],
                  16 * (2 * ncx + tg // 2))
                vr([], lambda ncx=ncx, tg=tg: v.tensor_tensor(
                    out=hTb[tg % 2][:], in0=silu_sb[tg % 2][:],
                    in1=pbank(4 + tg)[:, :], op=ALU.mult),
                   inc=(S.c_hT, 1))

        for ncx in range(NCH):
            deq_gu("g", ncx, S.c_wfg, S.c_dqg, False)
            if ncx > 0:
                ht_evac(ncx - 1)
            deq_gu("u", ncx, S.c_wfu, S.c_dqu, True)
            if ncx < NWU:
                deq_wd(ncx)
        deq_wd(NWU - 2)
        deq_wd(NWU - 1)
        ht_evac(NCH - 1)
        if ST < 4:
            return

        for b in range(2):
            for mg in range(8):
                for tt in range(8):
                    idx = (b * 8 + mg) * 8 + tt
                    v.wait_ge(S.c_ps3, idx + 1)
                    W(v, "v", [S.d_slab0, S.d_slab1][idx % 2], 16 * (idx // 2))
                    vr([], lambda tt=tt, idx=idx: v.tensor_copy(
                        out=slab[idx % 2][:], in_=pbank(tt)[:, :]),
                       inc=(S.c_slabv, 1))

        for t in range(NT):
            st4 = stats["s4"][t % 2]
            v.wait_ge([S.d_osr0, S.d_osr1][t % 2], 32 * (t // 2 + 1))
            W(v, "v", S.c_ln4, t - 1)
            t_s4 = vr([], lambda t=t: v.tensor_reduce(
                out=st4[:, 0:1], in_=osb[t % 2][:], axis=AX.X, op=ALU.add))
            vr([t_s4], lambda t=t: v.tensor_scalar(
                out=st4[:, 1:2], in0=st4[:, 0:1], scalar1=-1.0 / D, scalar2=None,
                op0=ALU.mult), inc=(S.c_v4, 1))
            v.wait_ge(S.c_ln4, t + 1)
            W(v, "v", [S.d_out0, S.d_out1][t % 2], 16 * (t // 2))
            vr([], lambda t=t: v.tensor_tensor(
                out=fout[t % 2][:], in0=x1c[t % 2][:], in1=ln4n[:], op=ALU.add),
               inc=(S.c_fo, 1))

    @blk.tensor
    def _(te):
        te.wait_ge(S.pg, 1)
        te.wait_ge(S.d_pro, 16 * 11)
        te.wait_ge(S.pv, 2)
        te.matmul(pbg[0][0:64, 0:1], onesr[0:1, 0:64], inv2_t[:],
                  start=True, stop=True)
        te.matmul(pbg[1][0:128, 0:1], onesr[0:1, :], dls_s[0:1, 0:1],
                  start=True, stop=True).then_inc(S.pt, 1)
        te.matmul(pbg[2][0:64, 0:64], onesr[0:1, 0:64], lam_s[0:1, 0:128:2],
                  start=True, stop=True)
        te.matmul(pbg[3][0:64, 0:64], onesr[0:1, 0:64], lam_s[0:1, 1:128:2],
                  start=True, stop=True).then_inc(S.pt, 1)
        te.transpose(pbg[4][0:64, 0:64], vc_s[:, 0:128:2], ident_f[0:64, 0:64])
        te.transpose(pbg[5][0:64, 0:64], vc_s[:, 1:128:2],
                     ident_f[0:64, 0:64]).then_inc(S.pt, 1)
        te.wait_ge(S.pv, 5)
        te.matmul(pbg[2][0:64, 0:1], vct[:, 0:64], bp_s[:], start=True, stop=True)
        te.matmul(pbg[3][0:64, 0:1], vct[:, 64:128], bp_s[:],
                  start=True, stop=True).then_inc(S.pt, 1)
        te.wait_ge(S.pv, 6)
        te.transpose(pbg[2][0:1, 0:64], bcol[:, 0:1], ident_f[0:64, 0:64])
        te.transpose(pbg[3][0:1, 0:64], bcol[:, 1:2],
                     ident_f[0:64, 0:64]).then_inc(S.pt, 1)
        te.wait_ge(S.pv, 7)
        te.matmul(pbg[6][0:64, 0:512], onesr[0:1, 0:64], vrow[0:1, :],
                  start=True, stop=True).then_inc(S.pt, 1)
        te.wait_ge(S.pv, 9)
        te.matmul(pbg[0][0:64, 0:128], c64[:], aru[:], start=True, stop=True)
        te.matmul(pbg[1][0:64, 0:128], s64[:], aru[:],
                  start=True, stop=True).then_inc(S.pt, 1)
        te.wait_ge(S.pv, 10)
        te.transpose(pbg[2][0:64, 0:64], zre[:], ident_f[0:64, 0:64])
        te.transpose(pbg[3][0:64, 0:64], zim[:],
                     ident_f[0:64, 0:64]).then_inc(S.pt, 1)
        te.wait_ge(S.pv, 11)
        te.matmul(pbg[4][0:64, 0:64], c64[:], zret[:], start=True, stop=False)
        te.matmul(pbg[4][0:64, 0:64], s64[:], zimtn[:],
                  start=False, stop=True).then_inc(S.pt, 1)
        if ST < 2:
            return

        te.wait_ge(S.pv, 12)
        if debug:
            te.wait_ge(S.d_zcw, 16 * dzw_n[0])
        for t in range(NT + 1):
            if 0 <= t - 1 < NT:
                tm = t - 1
                te.wait_ge(S.c_uTe, 4 * tm + 4)
                if tm == 0:
                    te.wait_ge(S.d_toep, 16 * 32)
                for h in range(2):
                    gs = [4 * h + gg for gg in range(4)]
                    for gg in range(4):
                        W(te, "te", S.c_yassb, 8 * tm + 4 * h + gg - 3)
                    kmax = 4 * gs[-1] + 3
                    for kcx in range(kmax + 1):
                        for gg, g_ in enumerate(gs):
                            if kcx > 4 * g_ + 3:
                                continue
                            inst = te.matmul(
                                pbank(4 + gg)[:, :],
                                uT[tm % 2][:, kcx * 128:(kcx + 1) * 128],
                                toep[:, (3 + 4 * g_ - kcx) * 128:
                                     (3 + 4 * g_ - kcx) * 128 + 512],
                                start=(kcx == 0), stop=(kcx == 4 * g_ + 3))
                            if kcx == kmax and g_ == gs[-1]:
                                inst.then_inc(S.c_conv, 1)
            if t < NT:
                te.wait_ge(S.c_ub, t + 1)
                for grp in range(4):
                    gi = 4 * t + grp
                    W(te, "te", S.c_uTe, gi - 1)
                    for c in range(8):
                        kcx = 8 * grp + c
                        inst = te.transpose(
                            pbank_bf(gi % 2)[:, c * 128:(c + 1) * 128],
                            ubf[t % 2][:, kcx * 128:(kcx + 1) * 128], ident_b[:])
                        if c == 7:
                            inst.then_inc(S.c_tr1a, 1)
        for t in range(NT):
            te.wait_ge(S.c_ln3, t + 1)
            for grp in range(4):
                gi = 4 * t + grp
                W(te, "te", S.c_xTe, gi - 1)
                for c in range(8):
                    cp_ = 8 * grp + c
                    jj, bb = cp_ >> 2, cp_ & 3
                    src = x1nb[t % 2][:, 1024 * bb + jj: 1024 * bb + jj + 1017: 8]
                    inst = te.transpose(
                        pbank_bf(gi % 2)[:, c * 128:(c + 1) * 128], src, ident_b[:])
                    if c == 7:
                        inst.then_inc(S.c_tr1b, 1)
        if ST < 3:
            return

        te.wait_ge(S.c_xTe, 4 * NT)
        te.wait_ge(S.c_yassb, 8 * NT)
        te.wait_ge(S.c_uTe, 4 * NT)
        for ncx in range(NCH):
            for m, c_dq, c_wf, c_free, c_ps, bb in (
                    ("g", S.c_dqg, S.c_wfg, S.c_silu, S.c_psg, 0),
                    ("u", S.c_dqu, S.c_wfu, S.c_hT, S.c_psu, 4)):
                te.wait_ge(c_dq, ncx + 1)
                for tg in range(4):
                    W(te, "te", c_free, 4 * (ncx - 1) + tg + 1)
                for cp_ in range(KC):
                    for tg in range(4):
                        inst = te.matmul(
                            pbank(bb + tg)[:, :],
                            wtil[m][:, cp_ * 128:(cp_ + 1) * 128],
                            xt[:, cp_ * TOK + tg * 512: cp_ * TOK + tg * 512 + 512],
                            start=(cp_ == 0), stop=(cp_ == KC - 1))
                        if cp_ == KC - 1:
                            inst.then_inc(c_ps if tg < 3 else c_wf, 1)
        if ST < 4:
            return

        te.wait_ge(S.c_silu, 4 * NCH)
        te.wait_ge(S.c_hT, 4 * NCH)
        for b in range(2):
            for mg in range(8):
                for grp in range(NGRP):
                    gidx = (b * 8 + mg) * NGRP + grp
                    te.wait_ge([S.d_wdr0, S.d_wdr1][gidx % 2],
                               16 * (gidx // 2 + 1))
                    for ci in range(4):
                        cd = grp * 4 + ci
                        kb, j, Kp = DCH[cd]
                        if mg == 0:
                            for q_ in range(min(cd // 11, 7) + 1):
                                W(te, "te", getattr(S, f"d_hr{q_}"),
                                  16 * 11 * (b + 1))
                        for tt in range(8):
                            if cd == 0:
                                W(te, "te", S.c_slabv,
                                  (b * 8 + mg - 1) * 8 + tt + 1)
                            inst = te.matmul(
                                pbank(tt)[:, :],
                                htblk[0:Kp, cd * 1024 + tt * 128:
                                      cd * 1024 + tt * 128 + 128],
                                wdbuf[grp % 2][0:Kp, ci * 512:(ci + 1) * 512],
                                start=(cd == 0), stop=(cd == NDCH - 1))
                            if cd == NDCH - 1:
                                inst.then_inc(S.c_ps3, 1)
                            elif ci == 3 and tt == 7:
                                inst.then_inc(S.c_mm3, 1)

    blk.__exit__(None, None, None)
    return nc


_NC_CACHE = {}


def kernel(**inputs):
    n = 8
    x = np.ascontiguousarray(np.asarray(inputs["L0_attn_u"], np.float32)
                             .reshape(16384, D))
    shards = np.split(x, n, axis=0)
    if 4 not in _NC_CACHE:
        _NC_CACHE[4] = build_program(stage=4, debug=False)
    nc = _NC_CACHE[4]
    base = {k: np.ascontiguousarray(np.asarray(inputs[k]).reshape(s), d)
            for k, s, d in [
                ("L0_attn_Lambda_ri", (64, 2), np.float32),
                ("L0_attn_p_ri", (64, 2), np.float32),
                ("L0_attn_q_ri", (64, 2), np.float32),
                ("L0_attn_Vc_ri", (64, 128), np.float32),
                ("L0_attn_Ct_ri", (1, 128), np.float32),
                ("L0_attn_Bp", (64, 1), np.float32),
                ("L0_attn_D", (1,), np.float32),
                ("L0_attn_log_step", (1,), np.float32),
                ("L0_ffn_gate_qweight", (512, INTER), np.int32),
                ("L0_ffn_gate_scales", (32, INTER), np.float32),
                ("L0_ffn_gate_qzeros", (32, 1376), np.int32),
                ("L0_ffn_gate_g_idx", (D,), np.int32),
                ("L0_ffn_up_qweight", (512, INTER), np.int32),
                ("L0_ffn_up_scales", (32, INTER), np.float32),
                ("L0_ffn_up_qzeros", (32, 1376), np.int32),
                ("L0_ffn_up_g_idx", (D,), np.int32),
                ("L0_ffn_down_qweight", (1376, D), np.int32),
                ("L0_ffn_down_scales", (86, D), np.float32),
                ("L0_ffn_down_qzeros", (86, 512), np.int32),
                ("L0_ffn_down_g_idx", (INTER,), np.int32),
            ]}
    in_maps = [dict(base, L0_attn_u=shards[i]) for i in range(n)]
    res = run_bass_kernel_spmd(nc, in_maps, core_ids=list(range(n)))
    outs = [np.asarray(r["out"]) for r in res.results]
    return np.concatenate(outs, axis=0).reshape(4, 4096, 4096)
